# version 09
# baseline (speedup 1.0000x reference)
import math
import numpy as np
import ml_dtypes
from contextlib import ExitStack

import concourse.bass as bass
import concourse.mybir as mybir
from concourse.bass_utils import run_bass_kernel_spmd

F32 = mybir.dt.float32
BF16 = mybir.dt.bfloat16
U8 = mybir.dt.uint8
ACT = mybir.ActivationFunctionType
ALU = mybir.AluOpType
AX = mybir.AxisListType.X
BF = ml_dtypes.bfloat16

B, S, E = 4, 2048, 4096
H, KVH, D = 32, 8, 128
TQ, TH = 1024, 512
EC, JT, UC = 32, 8, 8
NEG = -1.0e30
PT_OFF = [0, 128, 384, 768, 1280, 1920, 2688, 3584]
PT_TOT = 4608
DEBUG = False
SAFE_HEADS = False
SAFE_STAGES = False
COLL = True
PIPE_AB = True


def tile_w(WT):
    K, M = WT.shape
    Kc, Mc = K // 128, M // 128
    return np.ascontiguousarray(
        WT.reshape(Kc, 128, Mc, 128).transpose(2, 1, 0, 3).reshape(Mc, 128, K))


def tile_w64(WT):
    K, M = WT.shape
    Kc = K // 128
    return np.ascontiguousarray(WT.reshape(Kc, 128, M).transpose(1, 0, 2).reshape(128, Kc * M))


def rope_tables(positions, scale):
    inv_freq = 1.0 / (10000.0 ** (np.arange(0, D, 2, dtype=np.float64) / D))
    freqs = positions[:, None].astype(np.float64) * inv_freq[None, :]
    emb = np.concatenate([freqs, freqs], axis=-1)
    cos = (np.cos(emb).T * scale).astype(np.float32)
    sin = (np.sin(emb).T * scale).astype(np.float32)
    sin[:64] *= -1.0
    return np.ascontiguousarray(cos), np.ascontiguousarray(sin)


def perm_positions(h):
    if h == 0:
        return np.arange(0, 1024), np.arange(1024, 2048)
    q = (np.arange(1024, 2048).reshape(8, 128))[::-1].reshape(-1)
    kv = (np.arange(0, 1024).reshape(8, 128))[::-1].reshape(-1)
    return q, kv


class Sem:
    __slots__ = ("name", "n", "h")

    def __init__(self, name):
        self.name, self.n, self.h = name, 0, None


class Bank:
    __slots__ = ("idx", "guard")

    def __init__(self, idx):
        self.idx, self.guard = idx, None


class Plan:
    def __init__(self):
        self.ops = []
        self.sems = []
        self.tS = None
        self.hopA = None
        self.hopB = None
        self._hops = {}

    def sem(self, name):
        s = Sem(name)
        self.sems.append(s)
        return s

    def hop(self, v):
        if v in self._hops:
            return self._hops[v]
        tS, hB = self.tS, self.hopB
        assert v <= tS.n
        self.op("sync", lambda e, v=v: e.wait_ge(tS.h, v))
        self.op("sync", lambda e: e.nop(cycle_cnt=400), incs=[(hB, 1)])
        r = (hB, hB.n)
        self._hops[v] = r
        return r

    def op(self, eng, fn, waits=(), incs=()):
        ws = []
        for s, v in waits:
            assert v <= s.n, f"plan deadlock on {s.name}: wait {v} > planned {s.n}"
            if v > 0:
                if s is self.tS and eng in ("vector", "scalar"):
                    ws.append(self.hop(v))
                else:
                    ws.append((s, v))
        ics = []
        for s, a in incs:
            s.n += a
            ics.append((s, a))
        self.ops.append((eng, fn, ws, ics))


def build_nc():
    nc = bass.Bass("TRN2", target_bir_lowering=False)

    dri = lambda n, sh, dt=F32: nc.dram_tensor(n, sh, dt, kind="ExternalInput").ap()
    xq_h = dri("xq", [EC, 128, 1024])
    xkv_h = dri("xkv", [EC, 128, 1024])
    wkv_h = dri("wkv", [16, 128, E])
    wq_h = dri("wq", [32, 128, E])
    wo_h = dri("wo", [32, 128, E])
    wu1_h = dri("wu1", [8, 128, E])
    wfq_h = dri("wfq", [128, 2048])
    wvd_h = dri("wvd", [2, 128, E])
    wvu_h = dri("wvu", [32, 128, 256])
    wr_h = dri("wr", [64, 48])
    akt_h = dri("akt", [64, 1024])
    av_h = dri("av", [32, 128, 1024])
    w2r_h = dri("w2r", [8, 128, 128])
    bu1_h = dri("bu1c", [128, 8])
    lrel_h = dri("lrel", [2, 1024])
    bu2_h = dri("bu2r", [128, 1])
    gw1_h = dri("gw1r", [128, 1])
    gb_h = dri("gbr", [128, 1])
    cosq_h = dri("cosq", [128, 1024])
    sinq_h = dri("sinq", [128, 1024])
    cosk_h = dri("cosk", [128, 1024])
    sink_h = dri("sink", [128, 1024])
    eye_h = dri("eyec", [128, 128], BF16)
    ones_h = dri("onesc", [128, 128], BF16)
    p64_h = dri("p64c", [128, 128], BF16)
    mask_h = dri("maskc", [128, 128], BF16)
    ones1_h = dri("ones1c", [1, 512], BF16)

    y_h = nc.dram_tensor("y", [EC, 128, 1024], F32, kind="ExternalOutput").ap()
    dsm_h = nc.dram_tensor("dsm", [128, 16], F32, kind="ExternalOutput").ap()
    spill_r1 = nc.dram_tensor("spill_r1", [EC, 128, TH], F32, kind="Internal").ap()
    spill_h1 = nc.dram_tensor("spill_h1", [EC, 128, TH], F32, kind="Internal").ap()
    cc_in_t = nc.dram_tensor("cc_in", [128, 1], F32, kind="Internal")
    cc_out_t = nc.dram_tensor("cc_out", [128, 1], F32, kind="Internal")
    taps = {}
    if DEBUG:
        for name, sh in [("t_attn", [128, 32768]), ("t_qrope", [128, 32768]),
                         ("t_krope", [128, 8192]), ("t_vtm", [128, 8192]),
                         ("t_negc", [128, 32]), ("t_wun", [128, 1024]),
                         ("t_bot", [2, 128, 1024]), ("t_gate", [2, 128, 512]),
                         ("t_sums", [128, 1024])]:
            dt = F32 if name in ("t_negc", "t_gate", "t_sums") else BF16
            taps[name] = nc.dram_tensor(name, sh, dt, kind="ExternalOutput").ap()
        for name in ("t_r1", "t_h1", "t_r2"):
            taps[name] = nc.dram_tensor(name, [2, 128, 16384], F32, kind="ExternalOutput").ap()
        taps["t_pt0"] = nc.dram_tensor("t_pt0", [128, 4608], BF16, kind="ExternalOutput").ap()
        taps["t_sums0"] = nc.dram_tensor("t_sums0", [128, 2048], F32, kind="ExternalOutput").ap()
        taps["t_negc2"] = nc.dram_tensor("t_negc2", [128, 32], F32, kind="ExternalOutput").ap()
        taps["t_dbg"] = nc.dram_tensor("t_dbg", [128, 16], F32, kind="ExternalOutput").ap()
        taps["t_s1dbg"] = nc.dram_tensor("t_s1dbg", [128, 1024], F32, kind="ExternalOutput").ap()

    es = ExitStack()
    sb = lambda n, sh, dt: es.enter_context(nc.sbuf_tensor(n, sh, dt))
    ar1 = sb("ar1", [128, 32768], BF16)
    ar2 = sb("ar2", [128, 32768], BF16)
    ar2f = ar2[:].bitcast(F32)
    wbuf = sb("wbuf", [128, 8192], BF16)
    krope = sb("krope", [128, 8192], BF16)
    krf = krope[:].bitcast(F32)
    vtm = sb("vtm", [128, 8192], BF16)
    vtf = vtm[:].bitcast(F32)
    pt = sb("pt", [128, 9216], BF16)
    ptf = pt[:].bitcast(F32)
    ptu8 = pt[:].bitcast(U8)
    cs = sb("cs", [128, 2048], F32)
    csb = cs[:].bitcast(BF16)
    smalls = sb("smalls", [128, 16], F32)
    bu1_sb = sb("bu1sb", [128, 8], F32)
    t8_sb = sb("t8_sb", [128, 8], F32)
    eye = sb("eye", [128, 128], BF16)
    ones = sb("ones", [128, 128], BF16)
    p64 = sb("p64", [128, 128], BF16)
    maskt = sb("maskt", [128, 128], BF16)
    wr_sb = sb("wrsb", [64, 48], BF16)
    akaug = sb("akaug", [66, 1024], BF16)
    rkaug = sb("rkaug", [50, 1024], BF16)

    VTMP, KQSB = 0, 1024
    SCR1f, SCR2f = 1024, 2048
    SQBND = 6144
    SUMSf, RECIPf = 2304, 3328
    XQf, RBF, SQ2 = 0, 2048, 3072
    GSTGf = 2304
    MT, WFQ, W2R, GBF = 0, 4096, 6144, 7168
    MSKu8 = 16384
    rs_sb = vtf[:, 0:1024]
    a_sb = vtf[:, 1024:2048]
    wn_bf = vtm[:].bitcast(BF16)[:, 2048:3072]
    wun_sb = vtm[:].bitcast(BF16)[:, 0:1024]
    GATE = vtf[:, 2048:2560]
    M2 = vtf[:, 2560:3072]
    RSTD2 = vtf[:, 3072:3584]
    SCR = vtf[:, 3584:4096]
    msk_sb = ptu8[:, MSKu8:MSKu8 + 1024]
    def stats_sl(name, t):
        offs = {"m1": 0, "rstd1": 1024, "logvar": 2048, "sig": 3072}
        o = offs[name] + t * 512
        return krf[:, o:o + 512]
    botbf = csb[:, 0:2048]
    qfaug = csb[0:66, 2048:2560]
    rqaug = csb[0:50, 2560:3072]
    wbft = wbuf

    (SM_BU2, SM_GW1, SM_GB, SM_ZERO, SM_NEG, SM_NM, SM_SUM8, SM_REC8,
     SM_GM, SM_NVS, SM_T0, SM_T1) = range(12)
    SM_EPS, SM_C18, SM_CRQ, SM_N40 = 12, 13, 14, 15

    p = [es.enter_context(nc.psum_tensor(f"pb{i}", [128, 512], F32)) for i in range(8)]
    pbf = [pi[:].bitcast(BF16) for pi in p]
    banks = [Bank(i) for i in range(8)]

    pl = Plan()
    dW = pl.sem("dW")
    dX = pl.sem("dX")
    dO = pl.sem("dO")
    tS = pl.sem("tS")
    vS = pl.sem("vS")
    aS = pl.sem("aS")
    hA = pl.sem("hA")
    hB = pl.sem("hB")
    cS = pl.sem("cS")
    pl.tS, pl.hopA, pl.hopB = tS, hA, hB

    def stage_bar():
        if not SAFE_STAGES:
            return
        bar = [(tS, tS.n), (vS, vS.n), (aS, aS.n), (dW, dW.n), (dX, dX.n)]
        for en in ("tensor", "vector", "scalar", "gpsimd", "sync"):
            pl.op(en, None, waits=bar)

    def dma(q, out_ap, in_ap, sem, waits=()):
        pl.op(q, lambda e, o=out_ap, i=in_ap: e.dma_start(o, i), waits=waits, incs=[(sem, 16)])

    def bank_w(blist, waits=()):
        ws = list(waits)
        for b in blist:
            if b.guard is not None:
                ws.append(b.guard)
                b.guard = None
        return ws

    tap_guard = [None]

    def tap(name, src_ap, idx=None, waits=()):
        if not DEBUG:
            return
        dst = taps[name] if idx is None else taps[name][idx]
        dma("sync", dst, src_ap, dO, waits=waits)
        tap_guard[0] = (dO, dO.n)
        for en in ("tensor", "vector", "scalar", "gpsimd", "sync"):
            pl.op(en, None, waits=[(dO, dO.n)])

    def pop_tap():
        if tap_guard[0] is None:
            return []
        g = [tap_guard[0]]
        tap_guard[0] = None
        return g

    dma("sync", eye[:], eye_h, dX)
    dma("sync", ones[:], ones_h, dX)
    dma("sync", p64[:], p64_h, dX)
    dma("sync", maskt[:], mask_h, dX)
    dma("sync", cs[:, 0:1024], cosk_h, dX)
    dma("sync", cs[:, 1024:2048], sink_h, dX)
    dma("sync", bu1_sb[:], bu1_h, dX)
    dma("sync", smalls[:, SM_BU2:SM_BU2 + 1], bu2_h, dX)
    dma("sync", smalls[:, SM_GW1:SM_GW1 + 1], gw1_h, dX)
    dma("sync", smalls[:, SM_GB:SM_GB + 1], gb_h, dX)
    dma("gpsimd", akaug[0:64, :], akt_h, dW)
    dma("gpsimd", akaug[64:66, :], lrel_h, dW)
    dma("gpsimd", rkaug[48:50, :], lrel_h, dW)
    dma("gpsimd", wr_sb[:], wr_h, dW)
    pl.op("vector", lambda e: e.memset(smalls[:, SM_ZERO:SM_ZERO + 1], 0.0), incs=[(vS, 1)])
    pl.op("vector", lambda e: e.memset(smalls[:, SM_NEG:SM_NEG + 1], NEG), incs=[(vS, 1)])
    pl.op("vector", lambda e: e.memset(smalls[:, SM_EPS:SM_EPS + 1], 1e-5), incs=[(vS, 1)])
    pl.op("vector", lambda e: e.memset(smalls[:, SM_C18:SM_C18 + 1], 0.125), incs=[(vS, 1)])
    pl.op("vector", lambda e: e.memset(smalls[:, SM_CRQ:SM_CRQ + 1], 8.0 / math.sqrt(48.0)),
          incs=[(vS, 1)])
    pl.op("vector", lambda e: e.memset(smalls[:, SM_N40:SM_N40 + 1], -40.0), incs=[(vS, 1)])
    for fn_ in (ACT.Exp, ACT.Ln, ACT.Sqrt, ACT.Sigmoid, ACT.Gelu_apprx_tanh, ACT.Copy):
        pl.op("scalar", lambda e, fn_=fn_: e.activation(smalls[:, SM_NM:SM_NM + 1],
                                                        smalls[:, SM_ZERO:SM_ZERO + 1], fn_),
              waits=[(vS, vS.n)], incs=[(aS, 1)])
    consts_dX = dX.n
    consts_dW = dW.n

    wslot_guard = [None, None]

    def load_w(slot, in_ap, n_el=4096):
        ws = [] if wslot_guard[slot] is None else [wslot_guard[slot]]
        wslot_guard[slot] = None
        dma("gpsimd", wbuf[:, slot * 4096: slot * 4096 + n_el], in_ap, dW, waits=ws)
        return dW.n

    def proj_mm(bA, bB, wslot, wait_dw, x_dw, extra=()):
        ws = bank_w([bA, bB], waits=[(dW, wait_dw), (dW, x_dw)] + list(extra))
        for fc in range(EC):
            for hf in range(2):
                tgt = p[bA.idx] if hf == 0 else p[bB.idx]

                def mm(e, fc=fc, hf=hf, tgt=tgt, wslot=wslot):
                    return e.matmul(
                        tgt[:],
                        wbuf[:, wslot * 4096 + fc * 128: wslot * 4096 + (fc + 1) * 128],
                        ar1[:, fc * 1024 + hf * 512: fc * 1024 + (hf + 1) * 512],
                        start=(fc == 0), stop=(fc == EC - 1))
                pl.op("tensor", mm, waits=(ws if (fc == 0 and hf == 0) else ()),
                      incs=([(tS, 1)] if (fc == EC - 1 and hf == 1) else ()))
        return tS.n

    kq_guard = [None, None]
    KQOFF = [KQSB, VTMP]

    def rope_copy(pr, buf):
        kqoff = KQOFF[buf]
        ws0 = [(tS, tS.n)] + ([kq_guard[buf]] if kq_guard[buf] else [])
        kq_guard[buf] = None
        pl.op("scalar", lambda e, kqoff=kqoff, pr=pr:
              e.copy(pt[:, kqoff:kqoff + 512], p[pr[0].idx][:]),
              waits=ws0, incs=[(aS, 1)])
        pl.op("scalar", lambda e, kqoff=kqoff, pr=pr:
              e.copy(pt[:, kqoff + 512:kqoff + 1024], p[pr[1].idx][:]),
              incs=[(aS, 1)])
        return aS.n

    def rope_finish(pr, permp, buf, a_copy, out_buf, out_off):
        kqoff = KQOFF[buf]
        wsp = bank_w(list(permp), waits=[(aS, a_copy), (dX, consts_dX)])
        pl.op("tensor", lambda e, kqoff=kqoff, permp=permp:
              e.matmul(p[permp[0].idx][:], p64[:], pt[:, kqoff:kqoff + 512],
                       start=True, stop=True), waits=wsp)
        pl.op("tensor", lambda e, kqoff=kqoff, permp=permp:
              e.matmul(p[permp[1].idx][:], p64[:], pt[:, kqoff + 512:kqoff + 1024],
                       start=True, stop=True), incs=[(tS, 1)])
        t_perm = tS.n
        kq_guard[buf] = (tS, t_perm)
        for hf in range(2):
            pl.op("vector", lambda e, hf=hf, pr=pr:
                  e.tensor_mul(ptf[:, SCR1f + hf * 512: SCR1f + (hf + 1) * 512],
                               p[pr[hf].idx][:], cs[:, hf * 512:(hf + 1) * 512]),
                  waits=([(tS, t_perm)] if hf == 0 else ()), incs=[(vS, 1)])
        for hf in range(2):
            pl.op("vector", lambda e, hf=hf, permp=permp:
                  e.tensor_mul(ptf[:, SCR2f + hf * 512: SCR2f + (hf + 1) * 512],
                               p[permp[hf].idx][:], cs[:, 1024 + hf * 512: 1024 + (hf + 1) * 512]),
                  incs=[(vS, 1)])
        for hf in range(2):
            pl.op("vector", lambda e, hf=hf:
                  e.tensor_add(out_buf[:, out_off + hf * 512: out_off + (hf + 1) * 512],
                               ptf[:, SCR1f + hf * 512: SCR1f + (hf + 1) * 512],
                               ptf[:, SCR2f + hf * 512: SCR2f + (hf + 1) * 512]),
                  incs=[(vS, 1)])
        v_done = vS.n
        for b in tuple(pr) + tuple(permp):
            b.guard = (vS, v_done)
        return v_done

    for i in range(4):
        dma("gpsimd", ar1[:, i * 8192:(i + 1) * 8192],
            xkv_h[i * 8:(i + 1) * 8].transpose([1, 0, 2]), dW)
    x_dw, x_dx = dW.n, dX.n

    kv_order = []
    for g in range(8):
        kv_order += [g, 8 + g]
    wready = {}
    for idx in range(2):
        wready[kv_order[idx]] = load_w(idx % 2, wkv_h[kv_order[idx]])

    def v_evac(g, a_v):
        for hf in range(2):
            b6 = banks[6 + hf]
            wsb = bank_w([b6], waits=[(aS, a_v), (dX, consts_dX)])
            for j in range(4):
                def tr(e, hf=hf, j=j, b6=b6):
                    return e.transpose(pbf[b6.idx][:, j * 128:(j + 1) * 128],
                                       pt[:, VTMP + hf * 512 + j * 128:
                                          VTMP + hf * 512 + (j + 1) * 128],
                                       eye[:])
                pl.op("tensor", tr, waits=(wsb if j == 0 else ()),
                      incs=([(tS, 1)] if j == 3 else ()))
            if hf == 1:
                kq_guard[1] = (tS, tS.n)
            t_tr = tS.n
            dst_off = (g * 8 + hf * 4) * 128
            pl.op("vector", lambda e, dst_off=dst_off, b6=b6:
                  e.tensor_copy(vtm[:, dst_off:dst_off + 512], pbf[b6.idx][:, 0:512]),
                  waits=[(tS, t_tr)], incs=[(vS, 1)])
            b6.guard = (vS, vS.n)

    pendV = None
    for g in range(8):
        proj_mm(banks[0], banks[1], 0, wready[g], x_dw, extra=[(dX, x_dx)])
        wslot_guard[0] = (tS, tS.n)
        if 2 * g + 2 < 16:
            wready[kv_order[2 * g + 2]] = load_w(0, wkv_h[kv_order[2 * g + 2]])
        a_k = rope_copy((banks[0], banks[1]), 0)
        if pendV is not None and PIPE_AB:
            v_evac(*pendV)
        proj_mm(banks[2], banks[3], 1, wready[8 + g], x_dw, extra=[(dX, x_dx)])
        wslot_guard[1] = (tS, tS.n)
        if 2 * g + 3 < 16:
            wready[kv_order[2 * g + 3]] = load_w(1, wkv_h[kv_order[2 * g + 3]])
        t_v = tS.n
        ws = [(tS, t_v)] + ([kq_guard[1]] if kq_guard[1] else [])
        kq_guard[1] = None
        pl.op("scalar", lambda e: e.copy(pt[:, VTMP:VTMP + 512], p[banks[2].idx][:]),
              waits=ws, incs=[(aS, 1)])
        pl.op("scalar", lambda e: e.copy(pt[:, VTMP + 512:VTMP + 1024], p[banks[3].idx][:]),
              incs=[(aS, 1)])
        banks[2].guard = (aS, aS.n)
        banks[3].guard = (aS, aS.n)
        a_v = aS.n
        rope_finish((banks[0], banks[1]), (banks[4], banks[5]), 0, a_k, krope, g * 1024)
        pendV = (g, a_v)
        if not PIPE_AB:
            v_evac(*pendV)
            pendV = None
    if pendV is not None:
        v_evac(*pendV)

    cs_last_read = vS.n
    dma("sync", cs[:, 0:1024], cosq_h, dX, waits=[(vS, cs_last_read)])
    dma("sync", cs[:, 1024:2048], sinq_h, dX)
    csq_dx = dX.n
    tap("t_krope", krope[:], waits=[(vS, vS.n)])
    tap("t_vtm", vtm[:], waits=[(vS, vS.n)])

    t_a_done = tS.n
    for i in range(4):
        dma("gpsimd", ar1[:, i * 8192:(i + 1) * 8192],
            xq_h[i * 8:(i + 1) * 8].transpose([1, 0, 2]), dW,
            waits=([(tS, t_a_done)] if i == 0 else ()))
    x_dw, x_dx = dW.n, dX.n

    wready = {0: load_w(0, wq_h[0]), 1: load_w(1, wq_h[1])}
    pendQ = None
    for h in range(H):
        par = h % 2
        pr = (banks[0], banks[1]) if par == 0 else (banks[2], banks[3])
        extra = ([(dX, csq_dx)] if h == 0 else []) + [(dX, x_dx)]
        proj_mm(pr[0], pr[1], par, wready[h], x_dw, extra=extra)
        wslot_guard[par] = (tS, tS.n)
        if h + 2 < H:
            wready[h + 2] = load_w(par, wq_h[h + 2])
        a_copy = rope_copy(pr, par)
        if pendQ is not None and PIPE_AB:
            rope_finish(*pendQ)
        permp = (banks[4], banks[5]) if par == 0 else (banks[6], banks[7])
        pendQ = (pr, permp, par, a_copy, ar2, h * 1024)
        if not PIPE_AB:
            rope_finish(*pendQ)
            pendQ = None
    if pendQ is not None:
        rope_finish(*pendQ)
    tap("t_qrope", ar2[:], waits=[(vS, vS.n)])
    qrope_v = vS.n

    pt_guard = [wslot_guard[0], wslot_guard[1]]
    wslot_guard[0] = wslot_guard[1] = None
    sums_guard = [None]
    for h in range(H):
        g = h // 4
        buf = h % 2
        ptb = pt if buf == 0 else wbuf

        def qk_tiles(with_mask, exp_bias, exp_scale, first_waits):
            exp_marks = []
            ws_head = list(first_waits)
            for J in range(JT):
                span = (J + 1) * 128
                sbp = (banks[0], banks[1]) if J % 2 == 0 else (banks[2], banks[3])
                n0 = min(span, 512)
                ws = bank_w(list(sbp), waits=ws_head)
                ws_head = []
                pl.op("tensor", lambda e, g=g, J=J, n0=n0, b=sbp[0], h=h:
                      e.matmul(p[b.idx][:, 0:n0],
                               krope[:, g * 1024 + J * 128: g * 1024 + (J + 1) * 128],
                               ar2[:, h * 1024: h * 1024 + n0],
                               start=True, stop=(span <= 512 and not with_mask)),
                      waits=ws)
                if span > 512:
                    pl.op("tensor", lambda e, g=g, J=J, span=span, b=sbp[1], h=h:
                          e.matmul(p[b.idx][:, 0:span - 512],
                                   krope[:, g * 1024 + J * 128: g * 1024 + (J + 1) * 128],
                                   ar2[:, h * 1024 + 512: h * 1024 + span],
                                   start=True, stop=(not with_mask)))
                if with_mask:
                    dcol = span - 128
                    if dcol < 512:
                        tgt, off = sbp[0], dcol
                    else:
                        tgt, off = sbp[1], dcol - 512
                    pl.op("tensor", lambda e, tgt=tgt, off=off:
                          e.matmul(p[tgt.idx][:, off:off + 128], eye[:], maskt[:],
                                   start=False, stop=True))
                pl.op("tensor", lambda e: e.engine_nop() if False else None, waits=()) \
                    if False else None
                pl.ops[-1][3].append((tS, 0)) if False else None
                eng, fn, wss, ics = pl.ops[-1]
                tS.n += 1
                ics.append((tS, 1))
                t_sc = tS.n
                ws_exp = [(tS, t_sc)]
                if exp_marks == [] and pt_guard[buf] is not None:
                    ws_exp.append(pt_guard[buf])
                    pt_guard[buf] = None
                pl.op("scalar", lambda e, b=sbp[0], n0=n0, J=J, ptb=ptb, eb=exp_bias, sc=exp_scale:
                      e.activation(ptb[:, PT_OFF[J]: PT_OFF[J] + n0],
                                   p[b.idx][:, 0:n0], ACT.Exp, bias=eb, scale=sc),
                      waits=ws_exp, incs=[(aS, 1)])
                if span > 512:
                    pl.op("scalar", lambda e, b=sbp[1], span=span, J=J, ptb=ptb, eb=exp_bias, sc=exp_scale:
                          e.activation(ptb[:, PT_OFF[J] + 512: PT_OFF[J] + span],
                                       p[b.idx][:, 0:span - 512], ACT.Exp, bias=eb, scale=sc),
                          incs=[(aS, 1)])
                exp_marks.append(aS.n)
                sbp[0].guard = (aS, aS.n)
                sbp[1].guard = (aS, aS.n)
            return exp_marks

        def sums_mm(exp_marks):
            ws6 = bank_w([banks[6]], waits=[(aS, exp_marks[0])])
            ws7 = bank_w([banks[7]])
            for J in range(JT):
                span = (J + 1) * 128
                n0 = min(span, 512)
                pl.op("tensor", lambda e, J=J, n0=n0, ptb=ptb:
                      e.matmul(p[6][:, 0:n0], ones[:], ptb[:, PT_OFF[J]: PT_OFF[J] + n0],
                               start=(J == 0), stop=(J == JT - 1)),
                      waits=(ws6 if J == 0 else [(aS, exp_marks[J])]),
                      incs=([(tS, 1)] if J == JT - 1 else ()))
                if span > 512:
                    pl.op("tensor", lambda e, J=J, span=span, ptb=ptb:
                          e.matmul(p[7][:, 0:span - 512], ones[:],
                                   ptb[:, PT_OFF[J] + 512: PT_OFF[J] + span],
                                   start=(J == 4), stop=(J == JT - 1)),
                          waits=(ws7 if J == 4 else ()),
                          incs=([(tS, 1)] if J == JT - 1 else ()))
            return tS.n

        stage_bar()
        m2m = qk_tiles(True, smalls[:, SM_ZERO:SM_ZERO + 1], 1.0, [(vS, qrope_v)])
        stage_bar()
        t_s2 = sums_mm(m2m)
        stage_bar()
        ws = [(tS, t_s2)] + ([sums_guard[0]] if sums_guard[0] else [])
        sums_guard[0] = None
        pl.op("vector", lambda e: e.reciprocal(ptf[:, RECIPf:RECIPf + 512], p[6][:]),
              waits=ws, incs=[(vS, 1)])
        pl.op("vector", lambda e: e.reciprocal(ptf[:, RECIPf + 512:RECIPf + 1024], p[7][:]),
              incs=[(vS, 1)])
        banks[6].guard = (vS, vS.n)
        banks[7].guard = (vS, vS.n)
        recip_v = vS.n
        for qh in range(2):
            bav = banks[4 + qh]
            Js = list(range(JT)) if qh == 0 else list(range(4, JT))
            ws = bank_w([bav], waits=[(aS, m2m[-1])])
            for J in Js:
                span = (J + 1) * 128
                if qh == 0:
                    w = min(span, 512)
                    src_off = PT_OFF[J]
                else:
                    w = span - 512
                    src_off = PT_OFF[J] + 512
                pl.op("tensor", lambda e, g=g, J=J, w=w, src_off=src_off, bav=bav, ptb=ptb,
                      st=(J == Js[0]):
                      e.matmul(p[bav.idx][:, 0:w],
                               vtm[:, (g * 8 + J) * 128: (g * 8 + J + 1) * 128],
                               ptb[:, src_off: src_off + w],
                               start=st, stop=(J == JT - 1)),
                      waits=(ws if J == Js[0] else ()),
                      incs=([(tS, 1)] if J == JT - 1 else ()))
            t_av = tS.n
            pl.op("vector", lambda e, h=h, qh=qh, bav=bav:
                  e.tensor_mul(ar1[:, h * 1024 + qh * 512: h * 1024 + (qh + 1) * 512],
                               p[bav.idx][:], ptf[:, RECIPf + qh * 512: RECIPf + (qh + 1) * 512]),
                  waits=[(tS, t_av), (vS, recip_v)], incs=[(vS, 1)])
            bav.guard = (vS, vS.n)
        sums_guard[0] = (vS, vS.n)
        pt_guard[buf] = (tS, tS.n)
        if DEBUG and h == 0:
            tap("t_pt0", ptb[:, 0:4608], waits=[(vS, vS.n)])
            tap("t_sums0", ptf[:, 2304:4352], waits=[(vS, vS.n)])
        stage_bar()
        if SAFE_HEADS:
            bar = [(tS, tS.n), (vS, vS.n), (aS, aS.n)]
            for en in ("tensor", "vector", "scalar"):
                pl.op(en, None, waits=bar)
    attn_done_v = vS.n
    wslot_guard[0] = pt_guard[1]
    wslot_guard[1] = pt_guard[1]
    tap("t_attn", ar1[:], waits=[(vS, attn_done_v)])

    xq_guard = [None, None]
    rbf_guard = [None, None]
    mt_guard = [None]
    d_stats_t = [0]

    def phase_D(t, spill, first_waits=()):
        wready2 = {0: load_w(0, wo_h[0]), 1: load_w(1, wo_h[1])}
        xq_dx = {}
        for ec in range(2):
            ws = list(first_waits) if ec == 0 else []
            ws += [xq_guard[ec % 2]] if xq_guard[ec % 2] else []
            dma("sync", ptf[:, XQf + (ec % 2) * 512: XQf + (ec % 2 + 1) * 512],
                xq_h[ec, :, t * 512:(t + 1) * 512], dX, waits=ws)
            xq_dx[ec] = dX.n
        first_rbf = list(first_waits)
        RBF4, SQ4 = 2048, 4096
        v_sq_group = [0] * 8
        rbf4_guard = [None] * 4

        def stats_group(g):
            ws6 = bank_w([banks[6]], waits=[(vS, v_sq_group[g])])
            ws7 = bank_w([banks[7]])
            for j in range(4):
                ec = g * 4 + j
                pl.op("tensor", lambda e, j=j, ec=ec:
                      e.matmul(p[6][:], ones[:], pt[:, RBF4 + j * 512: RBF4 + (j + 1) * 512],
                               start=(ec == 0), stop=(ec == EC - 1)),
                      waits=(ws6 if j == 0 else []))
                pl.op("tensor", lambda e, j=j, ec=ec:
                      e.matmul(p[7][:], ones[:], pt[:, SQ4 + j * 512: SQ4 + (j + 1) * 512],
                               start=(ec == 0), stop=(ec == EC - 1)),
                      waits=(ws7 if j == 0 else []),
                      incs=([(tS, 1)] if j == 3 else ()))
            for j in range(4):
                rbf4_guard[j] = (tS, tS.n)

        for ec in range(EC):
            bwo = banks[ec % 4]
            ws = bank_w([bwo], waits=[(dW, wready2[ec]), (vS, attn_done_v)] +
                        (list(first_waits) if ec == 0 else []))
            for hd in range(H):
                def mm(e, hd=hd, ec=ec, t=t, wslot=ec % 2, bwo=bwo):
                    return e.matmul(
                        p[bwo.idx][:],
                        wbuf[:, wslot * 4096 + hd * 128: wslot * 4096 + (hd + 1) * 128],
                        ar1[:, hd * 1024 + t * 512: hd * 1024 + (t + 1) * 512],
                        start=(hd == 0), stop=(hd == H - 1))
                pl.op("tensor", mm, waits=(ws if hd == 0 else ()),
                      incs=([(tS, 1)] if hd == H - 1 else ()))
            wslot_guard[ec % 2] = (tS, tS.n)
            if ec + 2 < EC:
                wready2[ec + 2] = load_w(ec % 2, wo_h[ec + 2])
            if ec % 4 == 0 and ec >= 4:
                stats_group(ec // 4 - 1)
            t_wo = tS.n
            pl.op("vector", lambda e, ec=ec, bwo=bwo:
                  e.tensor_add(ar2f[:, ec * 512:(ec + 1) * 512], p[bwo.idx][:],
                               ptf[:, XQf + (ec % 2) * 512: XQf + (ec % 2 + 1) * 512]),
                  waits=[(tS, t_wo), (dX, xq_dx[ec])], incs=[(vS, 1)])
            bwo.guard = (vS, vS.n)
            xq_guard[ec % 2] = (vS, vS.n)
            if ec + 2 < EC:
                dma("sync", ptf[:, XQf + (ec % 2) * 512: XQf + (ec % 2 + 1) * 512],
                    xq_h[ec + 2, :, t * 512:(t + 1) * 512], dX, waits=[xq_guard[ec % 2]])
                xq_dx[ec + 2] = dX.n
            v_r1 = vS.n
            ws = list(first_rbf)
            first_rbf = []
            ws += [rbf4_guard[ec % 4]] if rbf4_guard[ec % 4] else []
            pl.op("vector", lambda e, ec=ec:
                  e.tensor_copy(pt[:, RBF4 + (ec % 4) * 512: RBF4 + (ec % 4 + 1) * 512],
                                ar2f[:, ec * 512:(ec + 1) * 512]),
                  waits=ws, incs=[(vS, 1)])
            rbf4_guard[ec % 4] = None
            pl.op("vector", lambda e, ec=ec:
                  e.tensor_mul(pt[:, SQ4 + (ec % 4) * 512: SQ4 + (ec % 4 + 1) * 512],
                               ar2f[:, ec * 512:(ec + 1) * 512],
                               ar2f[:, ec * 512:(ec + 1) * 512]),
                  incs=[(vS, 1)])
            v_sq_group[ec // 4] = vS.n
            if spill:
                dma("sync", spill_r1[ec], ar2f[:, ec * 512:(ec + 1) * 512], dO,
                    waits=[(vS, v_r1)])
            stage_bar()
        stats_group(7)
        d_stats_t[0] = tS.n
        t_stats = tS.n
        pl.op("vector", lambda e, t=t: e.tensor_scalar_mul(stats_sl("m1", t), p[6][:], 1.0 / E),
              waits=[(tS, t_stats)], incs=[(vS, 1)])
        pl.op("vector", lambda e: e.tensor_scalar_mul(SCR, p[7][:], 1.0 / E), incs=[(vS, 1)])
        banks[6].guard = (vS, vS.n)
        banks[7].guard = (vS, vS.n)
        pl.op("vector", lambda e, t=t: e.tensor_mul(RSTD2, stats_sl("m1", t), stats_sl("m1", t)),
              waits=[(vS, vS.n)], incs=[(vS, 1)])
        pl.op("vector", lambda e: e.tensor_sub(SCR, SCR, RSTD2),
              waits=[(vS, vS.n)], incs=[(vS, 1)])
        pl.op("vector", lambda e, t=t: e.tensor_copy(stats_sl("logvar", t), SCR),
              waits=[(vS, vS.n)], incs=[(vS, 1)])
        pl.op("scalar", lambda e: e.activation(SCR, SCR, ACT.Sqrt,
                                               bias=smalls[:, SM_EPS:SM_EPS + 1]),
              waits=[(vS, vS.n)], incs=[(aS, 1)])
        pl.op("vector", lambda e, t=t: e.reciprocal(stats_sl("rstd1", t), SCR),
              waits=[(aS, aS.n)], incs=[(vS, 1)])
        pl.op("vector", lambda e, t=t: e.tensor_mul(RSTD2, stats_sl("rstd1", t),
                                                    stats_sl("rstd1", t)),
              waits=[(vS, vS.n)], incs=[(vS, 1)])
        pl.op("vector", lambda e, t=t: e.tensor_mul(RSTD2, RSTD2, stats_sl("logvar", t)),
              waits=[(vS, vS.n)], incs=[(vS, 1)])
        pl.op("scalar", lambda e, t=t: e.activation(stats_sl("logvar", t), RSTD2, ACT.Ln, bias=1.0),
              waits=[(vS, vS.n)], incs=[(aS, 1)])
        pl.op("vector", lambda e, t=t:
              e.reduce_sum(smalls[:, SM_T0 + t:SM_T0 + t + 1], stats_sl("logvar", t), axis=AX),
              waits=[(aS, aS.n)], incs=[(vS, 1)])
        return vS.n

    h1bf_written = [0, 0]

    def phase_E(t, first_waits):
        fw = list(first_waits)
        if mt_guard[0] is not None:
            fw.append(mt_guard[0])
            mt_guard[0] = None
        marks = []
        for ec in range(EC):
            ws = fw if ec == 0 else []
            pl.op("vector", lambda e, ec=ec, t=t:
                  e.tensor_sub(SCR, ar2f[:, ec * 512:(ec + 1) * 512], stats_sl("m1", t)),
                  waits=ws, incs=[(vS, 1)])
            pl.op("vector", lambda e, ec=ec, t=t:
                  e.tensor_mul(ar2f[:, ec * 512:(ec + 1) * 512], SCR, stats_sl("rstd1", t)),
                  incs=[(vS, 1)])
            pl.op("vector", lambda e, ec=ec:
                  e.tensor_copy(ar1[:, ec * 512:(ec + 1) * 512],
                                ar2f[:, ec * 512:(ec + 1) * 512]),
                  incs=[(vS, 1)])
            marks.append(vS.n)
        h1bf_written[t] = vS.n
        return vS.n, marks

    rk_done = [None]
    aug_dx = [None]
    wun_guard = [None]
    botbf_guard = [None]

    def preload_F():
        dma("gpsimd", pt[:, WFQ:WFQ + 2048], wfq_h, dW,
            waits=[(vS, attn_done_v), (tS, d_stats_t[0])])
        dma("gpsimd", pt[:, W2R:W2R + 1024], w2r_h.transpose([1, 0, 2]), dW)
        wfq_dw = dW.n
        avr = {0: load_w(0, av_h[0:4].transpose([1, 0, 2])),
               1: load_w(1, av_h[4:8].transpose([1, 0, 2]))}
        return wfq_dw, avr

    def phase_F(t, pre=None):
        h1v = h1bf_written[t]
        wfq_dw, avr_pre = pre if pre is not None else preload_F()
        if rk_done[0] is None:
            dma("sync", qfaug[64:65, :], ones1_h, dX, waits=[(vS, attn_done_v)])
            dma("sync", qfaug[65:66, :], ones1_h, dX)
            dma("sync", rqaug[48:49, :], ones1_h, dX)
            dma("sync", rqaug[49:50, :], ones1_h, dX)
            aug_dx[0] = dX.n
            ws = bank_w([banks[0], banks[1]], waits=[(dW, consts_dW)])
            pl.op("tensor", lambda e: e.matmul(p[0][0:48, :], wr_sb[:], akaug[0:64, 0:512],
                                               start=True, stop=True), waits=ws)
            pl.op("tensor", lambda e: e.matmul(p[1][0:48, :], wr_sb[:], akaug[0:64, 512:1024],
                                               start=True, stop=True), incs=[(tS, 1)])
            pl.op("scalar", lambda e: e.copy(rkaug[0:48, 0:512], p[0][0:48, :]),
                  waits=[(tS, tS.n)], incs=[(aS, 1)])
            pl.op("scalar", lambda e: e.copy(rkaug[0:48, 512:1024], p[1][0:48, :]),
                  incs=[(aS, 1)])
            banks[0].guard = (aS, aS.n)
            banks[1].guard = (aS, aS.n)
            rk_done[0] = (aS, aS.n)
        b4 = banks[4]
        ws = bank_w([b4], waits=[(dW, wfq_dw), (vS, h1v)])
        for fc in range(EC):
            pl.op("tensor", lambda e, fc=fc, b4=b4:
                  e.matmul(p[b4.idx][0:64, :], pt[:, WFQ + fc * 64: WFQ + (fc + 1) * 64],
                           ar1[:, fc * 512:(fc + 1) * 512],
                           start=(fc == 0), stop=(fc == EC - 1)),
                  waits=(ws if fc == 0 else ()), incs=([(tS, 1)] if fc == EC - 1 else ()))
        ws = [(tS, tS.n)] + ([botbf_guard[0]] if botbf_guard[0] else [])
        botbf_guard[0] = None
        pl.op("scalar", lambda e: e.activation(qfaug[0:64, :], p[b4.idx][0:64, :],
                                               ACT.Copy, scale=smalls[0:64, SM_C18:SM_C18 + 1]),
              waits=ws, incs=[(aS, 1)])
        b4.guard = (aS, aS.n)
        qf_as = aS.n
        b5 = banks[5]
        ws = bank_w([b5], waits=[(aS, qf_as)])
        pl.op("tensor", lambda e, b5=b5: e.matmul(p[b5.idx][0:48, :], wr_sb[:], qfaug[0:64, :],
                                                  start=True, stop=True),
              waits=ws, incs=[(tS, 1)])
        pl.op("scalar", lambda e, b5=b5: e.activation(rqaug[0:48, :], p[b5.idx][0:48, :],
                                                      ACT.Copy, scale=smalls[0:48, SM_CRQ:SM_CRQ + 1]),
              waits=[(tS, tS.n)], incs=[(aS, 1)])
        b5.guard = (aS, aS.n)
        rq_as = aS.n
        for tt in range(4):
            pr = (banks[0], banks[1]) if tt % 2 == 0 else (banks[2], banks[3])
            ws = bank_w(list(pr), waits=[(aS, rq_as), rk_done[0], (dX, aug_dx[0])])
            for hf in range(2):
                pl.op("tensor", lambda e, tt=tt, hf=hf, b=pr[hf]:
                      e.matmul(p[b.idx][:], rqaug[:, tt * 128:(tt + 1) * 128],
                               rkaug[:, hf * 512:(hf + 1) * 512], start=True, stop=True),
                      waits=(ws if hf == 0 else ()), incs=([(tS, 1)] if hf == 1 else ()))
            t_rs = tS.n
            ws_rs = [(tS, t_rs)] + ([wun_guard[0]] if wun_guard[0] else [])
            wun_guard[0] = None
            for hf in range(2):
                pl.op("scalar", lambda e, hf=hf, b=pr[hf]:
                      e.copy(rs_sb[:, hf * 512:(hf + 1) * 512], p[b.idx][:]),
                      waits=(ws_rs if hf == 0 else ()), incs=[(aS, 1)])
            rs_as = aS.n
            pr[0].guard = (aS, rs_as)
            pr[1].guard = (aS, rs_as)
            ws = bank_w(list(pr), waits=[(aS, qf_as), (dX, aug_dx[0])])
            for hf in range(2):
                pl.op("tensor", lambda e, tt=tt, hf=hf, b=pr[hf]:
                      e.matmul(p[b.idx][:], qfaug[:, tt * 128:(tt + 1) * 128],
                               akaug[:, hf * 512:(hf + 1) * 512], start=True, stop=True),
                      waits=(ws if hf == 0 else ()), incs=([(tS, 1)] if hf == 1 else ()))
            t_lg = tS.n
            for hf in range(2):
                pl.op("scalar", lambda e, hf=hf, b=pr[hf]:
                      e.copy(a_sb[:, hf * 512:(hf + 1) * 512], p[b.idx][:]),
                      waits=([(tS, t_lg)] if hf == 0 else ()), incs=[(aS, 1)])
            a_as = aS.n
            pr[0].guard = (aS, a_as)
            pr[1].guard = (aS, a_as)
            pl.op("vector", lambda e: e.max(out=t8_sb[:], in_=rs_sb), waits=[(aS, rs_as)],
                  incs=[(vS, 1)])
            pl.op("vector", lambda e: e.tensor_scalar(msk_sb, rs_sb, t8_sb[:, 7:8],
                                                      None, ALU.is_ge),
                  waits=[(vS, vS.n)], incs=[(vS, 1)])
            pl.op("vector", lambda e: e.select(rs_sb, msk_sb, a_sb,
                                               smalls[:, SM_NEG:SM_NEG + 1].to_broadcast([128, 1024])),
                  waits=[(aS, a_as), (vS, vS.n)], incs=[(vS, 1)])
            pl.op("scalar", lambda e: e.activation(wn_bf, rs_sb, ACT.Exp,
                                                   bias=smalls[:, SM_ZERO:SM_ZERO + 1],
                                                   accum_out=smalls[:, SM_SUM8:SM_SUM8 + 1]),
                  waits=[(vS, vS.n)], incs=[(aS, 1)])
            pl.op("vector", lambda e: e.reciprocal(smalls[:, SM_REC8:SM_REC8 + 1],
                                                   smalls[:, SM_SUM8:SM_SUM8 + 1]),
                  waits=[(aS, aS.n)], incs=[(vS, 1)])
            pl.op("vector", lambda e: e.tensor_scalar_mul(wun_sb, wn_bf,
                                                          smalls[:, SM_REC8:SM_REC8 + 1]),
                  waits=[(vS, vS.n)], incs=[(vS, 1)])
            v_w = vS.n
            if DEBUG and t == 1 and tt == 0:
                tap("t_wun", wun_sb, waits=[(vS, v_w)])
            for grp in range(2):
                b7 = banks[6 + grp]
                wsb = bank_w([b7], waits=[(vS, v_w)])
                for j in range(4):
                    nck = grp * 4 + j
                    pl.op("tensor", lambda e, nck=nck, j=j, b7=b7:
                          e.transpose(pbf[b7.idx][:, j * 128:(j + 1) * 128],
                                      wun_sb[:, nck * 128:(nck + 1) * 128], eye[:]),
                          waits=(wsb if j == 0 else ()),
                          incs=([(tS, 1)] if j == 3 else ()))
                t_tr = tS.n
                wun_guard[0] = (tS, t_tr)

                def ev(e, grp=grp, tt=tt, b7=b7):
                    dst = pt[:, MT:MT + 4096].rearrange("p (nck q) -> p nck q", q=512)
                    return e.copy(dst[:, grp * 4:(grp + 1) * 4, tt * 128:(tt + 1) * 128],
                                  pbf[b7.idx][:, 0:512])
                pl.op("scalar", ev, waits=[(tS, t_tr)], incs=[(aS, 1)])
                b7.guard = (aS, aS.n)
        mt_as = aS.n
        avr = dict(avr_pre)
        aux_as_last = None
        for ec in range(EC):
            grp, within = ec // 4, ec % 4
            bax = banks[ec % 4]
            ws = bank_w([bax], waits=[(dW, avr[grp % 2]), (aS, mt_as)])
            for nck in range(8):
                def mm(e, nck=nck, within=within, grp=grp, bax=bax):
                    off = (grp % 2) * 4096 + within * 1024 + nck * 128
                    return e.matmul(p[bax.idx][:], wbuf[:, off:off + 128],
                                    pt[:, MT + nck * 512: MT + nck * 512 + 512],
                                    start=(nck == 0), stop=(nck == 7))
                pl.op("tensor", mm, waits=(ws if nck == 0 else ()),
                      incs=([(tS, 1)] if nck == 7 else ()))
            if within == 3:
                wslot_guard[grp % 2] = (tS, tS.n)
                if grp + 2 < 8:
                    avr[grp % 2] = load_w(
                        grp % 2, av_h[(grp + 2) * 4:(grp + 3) * 4].transpose([1, 0, 2]))
            pl.op("scalar", lambda e, ec=ec, bax=bax:
                  e.copy(ar1[:, 16384 + ec * 512: 16384 + (ec + 1) * 512], p[bax.idx][:]),
                  waits=[(tS, tS.n)], incs=[(aS, 1)])
            bax.guard = (aS, aS.n)
            aux_as_last = aS.n
        mt_guard[0] = (tS, tS.n)
        wvd_dw = {0: load_w(0, wvd_h[0]), 1: load_w(1, wvd_h[1])}
        for m in range(2):
            bvd = banks[4 + m]
            ws = bank_w([bvd], waits=[(dW, wvd_dw[m]), (aS, aux_as_last)])
            for ec in range(EC):
                def mm(e, ec=ec, m=m, bvd=bvd):
                    return e.matmul(p[bvd.idx][:],
                                    wbuf[:, m * 4096 + ec * 128: m * 4096 + (ec + 1) * 128],
                                    ar1[:, 16384 + ec * 512: 16384 + (ec + 1) * 512],
                                    start=(ec == 0), stop=(ec == EC - 1))
                pl.op("tensor", mm, waits=(ws if ec == 0 else ()),
                      incs=([(tS, 1)] if ec == EC - 1 else ()))
            wslot_guard[m] = (tS, tS.n)
            pl.op("scalar", lambda e, m=m, t=t, bvd=bvd:
                  e.copy(botbf[:, t * 1024 + m * 512: t * 1024 + (m + 1) * 512], p[bvd.idx][:]),
                  waits=[(tS, tS.n)], incs=[(aS, 1)])
            bvd.guard = (aS, aS.n)
        if DEBUG:
            tap("t_bot", botbf[:, t * 1024:(t + 1) * 1024], idx=t, waits=[(aS, aS.n)])
        wu_dw = {0: load_w(0, wu1_h[0]), 1: load_w(1, wu1_h[1])}
        b6 = banks[6]
        ws_l = bank_w([b6])
        gbf_guard = [None, None]
        for uc in range(UC):
            bg = banks[uc % 2]
            ws = bank_w([bg], waits=[(dW, wu_dw[uc]), (vS, h1bf_written[t])])
            for fc in range(EC):
                def mm(e, fc=fc, uc=uc, bg=bg):
                    return e.matmul(p[bg.idx][:],
                                    wbuf[:, (uc % 2) * 4096 + fc * 128: (uc % 2) * 4096 + (fc + 1) * 128],
                                    ar1[:, fc * 512:(fc + 1) * 512],
                                    start=(fc == 0), stop=(fc == EC - 1))
                pl.op("tensor", mm, waits=(ws if fc == 0 else ()),
                      incs=([(tS, 1)] if fc == EC - 1 else ()))
            wslot_guard[uc % 2] = (tS, tS.n)
            if uc + 2 < UC:
                wu_dw[uc + 2] = load_w(uc % 2, wu1_h[uc + 2])
            t_u = tS.n
            ws = [gbf_guard[uc % 2]] if gbf_guard[uc % 2] else []
            pl.op("scalar", lambda e, uc=uc, bg=bg:
                  e.activation(pt[:, GBF + (uc % 2) * 512: GBF + (uc % 2 + 1) * 512],
                               p[bg.idx][:], ACT.Gelu_apprx_tanh, bias=bu1_sb[:, uc:uc + 1]),
                  waits=[(tS, t_u)] + ws, incs=[(aS, 1)])
            bg.guard = (aS, aS.n)
            a_g = aS.n
            pl.op("tensor", lambda e, uc=uc, b6=b6:
                  e.matmul(p[b6.idx][:], pt[:, W2R + uc * 128: W2R + (uc + 1) * 128],
                           pt[:, GBF + (uc % 2) * 512: GBF + (uc % 2 + 1) * 512],
                           start=(uc == 0), stop=(uc == UC - 1)),
                  waits=(ws_l if uc == 0 else []) + [(aS, a_g), (dW, wfq_dw)],
                  incs=([(tS, 1)] if uc == UC - 1 else ()))
            gbf_guard[uc % 2] = (tS, tS.n)
        pl.op("scalar", lambda e, t=t:
              e.activation(stats_sl("sig", t), p[b6.idx][:], ACT.Sigmoid,
                           bias=smalls[:, SM_BU2:SM_BU2 + 1], scale=1.0),
              waits=[(tS, tS.n), (dX, consts_dX)], incs=[(aS, 1)])
        b6.guard = (aS, aS.n)
        return aS.n

    gm_local = [None]

    def gmean_start(t0_v):
        pl.op("vector", lambda e:
              e.tensor_add(smalls[:, SM_GM:SM_GM + 1], smalls[:, SM_T0:SM_T0 + 1],
                           smalls[:, SM_T1:SM_T1 + 1]),
              waits=[(vS, t0_v)], incs=[(vS, 1)])
        gm_local[0] = [(vS, vS.n)]
        if not COLL:
            return
        dma("sync", cc_in_t[:, :], smalls[:, SM_GM:SM_GM + 1], dO, waits=[(vS, vS.n)])
        cc_in_d = dO.n
        pl.op("gpsimd", lambda e: e.collective_compute(
            "AllReduce", mybir.AluOpType.add, replica_groups=[list(range(8))],
            ins=[cc_in_t.ap().opt()], outs=[cc_out_t.ap().opt()]),
            waits=[(dO, cc_in_d)], incs=[(cS, 1)])

    def gmean_fetch():
        if not COLL:
            return gm_local[0]
        pl.op("gpsimd", lambda e: e.nop(cycle_cnt=2400), waits=[(cS, 1)], incs=[(cS, 1)])
        dma("gpsimd", smalls[:, SM_GM:SM_GM + 1], cc_out_t[:, :], dX, waits=[(cS, 2)])
        return [(dX, dX.n)]

    def gmean_combine(gm_ready):
        denom = 8192.0 if COLL else 1024.0
        pl.op("vector", lambda e, denom=denom:
              e.tensor_scalar(smalls[:, SM_GM:SM_GM + 1], smalls[:, SM_GM:SM_GM + 1],
                              1.0 / denom, 1e-6, ALU.mult, ALU.add),
              waits=gm_ready, incs=[(vS, 1)])
        pl.op("vector", lambda e: e.memset(smalls[:, SM_T0:SM_T0 + 1], 0.0),
              waits=[(vS, vS.n)], incs=[(vS, 1)])
        pl.op("vector", lambda e:
              e.reciprocal(smalls[:, SM_NVS:SM_NVS + 1], smalls[:, SM_GM:SM_GM + 1]),
              waits=[(vS, vS.n)], incs=[(vS, 1)])
        pl.op("vector", lambda e:
              e.tensor_scalar_mul(smalls[:, SM_NVS:SM_NVS + 1],
                                  smalls[:, SM_NVS:SM_NVS + 1], 0.5),
              waits=[(vS, vS.n)], incs=[(vS, 1)])
        return vS.n

    def phase_G(t, nv_v, h1_ready, bot_ready=(), g_first=()):
        pl.op("vector", lambda e, t=t:
              e.tensor_scalar_mul(SCR, stats_sl("logvar", t), smalls[:, SM_NVS:SM_NVS + 1]),
              waits=[(vS, nv_v)] + list(g_first), incs=[(vS, 1)])
        pl.op("vector", lambda e, t=t:
              e.scalar_tensor_tensor(GATE, stats_sl("sig", t), 2.5, SCR, ALU.mult, ALU.add),
              waits=[(vS, vS.n)], incs=[(vS, 1)])
        pl.op("vector", lambda e:
              e.tensor_scalar(GATE, GATE, 0.0, 5.0, ALU.max, ALU.min),
              waits=[(vS, vS.n)], incs=[(vS, 1)])
        pl.op("vector", lambda e:
              e.tensor_scalar(GATE, GATE, -0.5, 1.0 / 1.5, ALU.add, ALU.mult),
              waits=[(vS, vS.n)], incs=[(vS, 1)])
        pl.op("vector", lambda e:
              e.tensor_scalar(GATE, GATE, 0.0, 1.0, ALU.max, ALU.min),
              waits=[(vS, vS.n)], incs=[(vS, 1)])
        pl.op("scalar", lambda e:
              e.activation(GATE, GATE, ACT.Sigmoid, bias=smalls[:, SM_GB:SM_GB + 1],
                           scale=smalls[:, SM_GW1:SM_GW1 + 1]),
              waits=[(vS, vS.n)], incs=[(aS, 1)])
        pl.op("vector", lambda e:
              e.tensor_scalar(msk_sb[:, 0:512], GATE, 0.05, None, ALU.is_ge),
              waits=[(aS, aS.n)], incs=[(vS, 1)])
        pl.op("vector", lambda e:
              e.select(GATE, msk_sb[:, 0:512], GATE,
                       smalls[:, SM_ZERO:SM_ZERO + 1].to_broadcast([128, 512])),
              waits=[(vS, vS.n)], incs=[(vS, 1)])
        gate_v = vS.n
        if DEBUG:
            tap("t_gate", GATE, idx=t, waits=[(vS, gate_v)])
        for kc in range(2):
            pl.op("vector", lambda e, kc=kc, t=t:
                  e.tensor_mul(botbf[:, t * 1024 + kc * 512: t * 1024 + (kc + 1) * 512],
                               botbf[:, t * 1024 + kc * 512: t * 1024 + (kc + 1) * 512], GATE),
                  waits=([(vS, gate_v)] + list(bot_ready)) if kc == 0 else (),
                  incs=[(vS, 1)])
        botg_v = vS.n
        wvur = {0: load_w(0, wvu_h[0:8].transpose([1, 0, 2]), n_el=2048),
                1: load_w(1, wvu_h[8:16].transpose([1, 0, 2]), n_el=2048)}
        sb6 = bank_w([banks[6]])
        sb7 = bank_w([banks[7]])
        for ec in range(EC):
            grp, within = ec // 8, ec % 8
            bvu = banks[ec % 4]
            ws = bank_w([bvu], waits=[(dW, wvur[grp % 2])] +
                        ([(vS, h1_ready), (vS, botg_v)] if ec == 0 else []))
            for kc in range(2):
                def mm(e, ec=ec, kc=kc, t=t, bvu=bvu, grp=grp, within=within):
                    off = (grp % 2) * 4096 + within * 256 + kc * 128
                    return e.matmul(p[bvu.idx][:], wbuf[:, off:off + 128],
                                    botbf[:, t * 1024 + kc * 512: t * 1024 + (kc + 1) * 512],
                                    start=(kc == 0), stop=(kc == 1))
                pl.op("tensor", mm, waits=(ws if kc == 0 else ()),
                      incs=([(tS, 1)] if kc == 1 else ()))
            if within == 7:
                wslot_guard[grp % 2] = (tS, tS.n)
                if grp + 2 < 4:
                    wvur[grp % 2] = load_w(
                        grp % 2, wvu_h[(grp + 2) * 8:(grp + 3) * 8].transpose([1, 0, 2]), n_el=2048)
            t_vu = tS.n
            pl.op("vector", lambda e, ec=ec, bvu=bvu:
                  e.tensor_add(ar2f[:, ec * 512:(ec + 1) * 512],
                               ar2f[:, ec * 512:(ec + 1) * 512], p[bvu.idx][:]),
                  waits=[(tS, t_vu)], incs=[(vS, 1)])
            bvu.guard = (vS, vS.n)
            ws = [rbf_guard[ec % 2]] if rbf_guard[ec % 2] else []
            pl.op("vector", lambda e, ec=ec:
                  e.tensor_copy(pt[:, RBF + (ec % 2) * 512: RBF + (ec % 2 + 1) * 512],
                                ar2f[:, ec * 512:(ec + 1) * 512]),
                  waits=ws, incs=[(vS, 1)])
            pl.op("vector", lambda e, ec=ec:
                  e.tensor_mul(pt[:, SQ2 + (ec % 2) * 512: SQ2 + (ec % 2 + 1) * 512],
                               ar2f[:, ec * 512:(ec + 1) * 512],
                               ar2f[:, ec * 512:(ec + 1) * 512]),
                  incs=[(vS, 1)])
            v_sq = vS.n
            pl.op("tensor", lambda e, ec=ec:
                  e.matmul(p[6][:], ones[:], pt[:, RBF + (ec % 2) * 512: RBF + (ec % 2 + 1) * 512],
                           start=(ec == 0), stop=(ec == EC - 1)),
                  waits=(sb6 if ec == 0 else []) + [(vS, v_sq)])
            pl.op("tensor", lambda e, ec=ec:
                  e.matmul(p[7][:], ones[:], pt[:, SQ2 + (ec % 2) * 512: SQ2 + (ec % 2 + 1) * 512],
                           start=(ec == 0), stop=(ec == EC - 1)),
                  waits=(sb7 if ec == 0 else []),
                  incs=[(tS, 1)])
            rbf_guard[ec % 2] = (tS, tS.n)
        t_stats = tS.n
        if DEBUG:
            tap("t_r2", ar2f[:], idx=t, waits=[(vS, vS.n)])
        pl.op("vector", lambda e: e.tensor_scalar_mul(M2, p[6][:], 1.0 / E),
              waits=[(tS, t_stats)], incs=[(vS, 1)])
        pl.op("vector", lambda e: e.tensor_scalar_mul(SCR, p[7][:], 1.0 / E), incs=[(vS, 1)])
        banks[6].guard = (vS, vS.n)
        banks[7].guard = (vS, vS.n)
        pl.op("vector", lambda e: e.tensor_mul(RSTD2, M2, M2),
              waits=[(vS, vS.n)], incs=[(vS, 1)])
        pl.op("vector", lambda e: e.tensor_sub(SCR, SCR, RSTD2),
              waits=[(vS, vS.n)], incs=[(vS, 1)])
        pl.op("scalar", lambda e: e.activation(SCR, SCR, ACT.Sqrt,
                                               bias=smalls[:, SM_EPS:SM_EPS + 1]),
              waits=[(vS, vS.n)], incs=[(aS, 1)])
        pl.op("vector", lambda e: e.reciprocal(RSTD2, SCR), waits=[(aS, aS.n)], incs=[(vS, 1)])
        ln2_w = (vS, vS.n)
        stg_guard = [None, None]
        sub_marks = []
        for ec in range(EC):
            ws = [ln2_w] if ec == 0 else []
            pl.op("vector", lambda e, ec=ec:
                  e.tensor_sub(SCR, ar2f[:, ec * 512:(ec + 1) * 512], M2),
                  waits=ws, incs=[(vS, 1)])
            sub_marks.append(vS.n)
            ws = [stg_guard[ec % 2]] if stg_guard[ec % 2] else []
            pl.op("vector", lambda e, ec=ec:
                  e.tensor_mul(ptf[:, GSTGf + (ec % 2) * 512: GSTGf + (ec % 2 + 1) * 512],
                               SCR, RSTD2),
                  waits=ws, incs=[(vS, 1)])
            v_o = vS.n
            dma("sync", y_h[ec, :, t * 512:(t + 1) * 512],
                ptf[:, GSTGf + (ec % 2) * 512: GSTGf + (ec % 2 + 1) * 512], dO,
                waits=[(vS, v_o)])
            stg_guard[ec % 2] = (dO, dO.n)
        return sub_marks

    r1v_0 = phase_D(0, spill=True, first_waits=[(vS, attn_done_v)] + pop_tap())
    spill_r1_dO = dO.n
    r1v_1 = phase_D(1, spill=False, first_waits=[(dO, spill_r1_dO)])
    gmean_start(r1v_1)
    if DEBUG:
        tap("t_r1", ar2f[:], idx=1, waits=[(vS, r1v_1)])
    h1v_1, e1_marks = phase_E(1, [(vS, r1v_1)] + pop_tap())
    if DEBUG:
        tap("t_h1", ar2f[:], idx=1, waits=[(vS, h1v_1)])
    _tg = pop_tap()
    sp_d = []
    for i in range(8):
        dma("sync", spill_h1[i * 4:(i + 1) * 4].transpose([1, 0, 2]),
            ar2f[:, i * 2048:(i + 1) * 2048], dO,
            waits=[(vS, e1_marks[i * 4 + 3])] + (_tg if i == 0 else []))
        sp_d.append(dO.n)
    pre1 = preload_F()
    for i in range(8):
        dma("gpsimd", ar2f[:, i * 2048:(i + 1) * 2048],
            spill_r1[i * 4:(i + 1) * 4].transpose([1, 0, 2]), dX,
            waits=[(dO, sp_d[i])])
    r1_reload = dX.n
    f1_done = phase_F(1, pre=pre1)
    if DEBUG:
        tap("t_r1", ar2f[:], idx=0, waits=[(dX, r1_reload)])
    h1v_0, _ = phase_E(0, [(dX, r1_reload), (vS, r1v_0)] + pop_tap())
    if DEBUG:
        tap("t_h1", ar2f[:], idx=0, waits=[(vS, h1v_0)])
    _tg0 = pop_tap()
    f0_done = phase_F(0)
    _ = _tg0
    _tgG = pop_tap()
    gm_ready = gmean_fetch()
    nv_v = gmean_combine(gm_ready)
    g0_subs = phase_G(0, nv_v, h1v_0, bot_ready=[(aS, f0_done)], g_first=_tgG)
    for i in range(8):
        dma("gpsimd", ar2f[:, i * 2048:(i + 1) * 2048],
            spill_h1[i * 4:(i + 1) * 4].transpose([1, 0, 2]), dX,
            waits=[(vS, g0_subs[i * 4 + 3])])
    h1_reload = dX.n
    pl.op("vector", lambda e: e.memset(smalls[:, SM_NM:SM_NM + 1], 0.0),
          waits=[(dX, h1_reload)], incs=[(vS, 1)])
    phase_G(1, nv_v, vS.n, bot_ready=[(aS, f1_done)], g_first=pop_tap())

    dma("sync", dsm_h, smalls[:], dO, waits=[(vS, vS.n), (aS, aS.n)])
    pl.op("sync", None,
          waits=[(dO, dO.n), (dW, dW.n), (dX, dX.n), (tS, tS.n), (vS, vS.n), (aS, aS.n)])

    with (nc.semaphore("sdW") as h_dW, nc.semaphore("sdX") as h_dX,
          nc.semaphore("sdO") as h_dO, nc.semaphore("stS") as h_tS,
          nc.semaphore("svS") as h_vS, nc.semaphore("saS") as h_aS,
          nc.semaphore("shA") as h_hA, nc.semaphore("shB") as h_hB,
          nc.semaphore("scS") as h_cS):
        handles = {"dW": h_dW, "dX": h_dX, "dO": h_dO, "tS": h_tS, "vS": h_vS, "aS": h_aS,
                   "hA": h_hA, "hB": h_hB, "cS": h_cS}
        for s in pl.sems:
            s.h = handles[s.name]
        with nc.Block() as block:
            def make(eng_name):
                def body(eng):
                    for e_name, fn, waits, incs in pl.ops:
                        if e_name != eng_name:
                            continue
                        for s, v in waits:
                            eng.wait_ge(s.h, v)
                        if fn is None:
                            continue
                        inst = fn(eng)
                        for s, a in incs:
                            inst = inst.then_inc(s.h, a)
                return body
            block.sync(make("sync"))
            block.gpsimd(make("gpsimd"))
            block.tensor(make("tensor"))
            block.vector(make("vector"))
            block.scalar(make("scalar"))
    es.close()
    return nc


_NC_CACHE = [None]


def _prep_shared(wd):
    out = {}
    out["wkv"] = np.concatenate([tile_w(wd["Wk"].T), tile_w(wd["Wv"].T)], axis=0)
    out["wq"] = tile_w(wd["Wq"].T)
    out["wo"] = tile_w(wd["Wo"].T)
    out["wu1"] = tile_w(wd["Wu1"].T)
    out["wfq"] = tile_w64(wd["Wfq"].T)
    out["wvd"] = tile_w(wd["Wvd"].T)
    out["wvu"] = tile_w(wd["Wvu"].T)
    out["wr"] = np.ascontiguousarray(wd["Wr"].T)
    out["akt"] = np.ascontiguousarray(wd["ak"].T)
    out["av"] = tile_w(wd["avv"])
    out["w2r"] = np.ascontiguousarray(np.tile(wd["Wu2"].reshape(8, 128, 1), (1, 1, 128)))
    out["bu1c"] = np.ascontiguousarray(wd["bu1"].reshape(8, 128).T)
    lr = wd["lrel"].reshape(1, 1024)
    lr_hi = lr.astype(BF).astype(np.float32)
    out["lrel"] = np.ascontiguousarray(np.concatenate([lr_hi, lr - lr_hi], axis=0))
    out["bu2r"] = np.tile(wd["bu2"].reshape(1, 1), (128, 1)).astype(np.float32)
    out["gw1r"] = np.full((128, 1), np.float32(wd["gw1"]), dtype=np.float32)
    out["gbr"] = np.full((128, 1), np.float32(wd["gb"]), dtype=np.float32)
    out["eyec"] = np.eye(128, dtype=np.float32).astype(BF)
    out["onesc"] = np.ones((128, 128), dtype=BF)
    p64_np = np.zeros((128, 128), dtype=np.float32)
    p64_np[(np.arange(128) + 64) % 128, np.arange(128)] = 1.0
    out["p64c"] = p64_np.astype(BF)
    out["ones1c"] = np.ones((1, 512), dtype=BF)
    return out


def _prep_core(x, b, h):
    posq, poskv = perm_positions(h)
    d = {}
    xb = x[b]
    d["xq"] = np.ascontiguousarray(xb[posq].T.reshape(EC, 128, 1024))
    d["xkv"] = np.ascontiguousarray(xb[poskv].T.reshape(EC, 128, 1024))
    d["cosq"], d["sinq"] = rope_tables(posq, 1.0 / math.sqrt(D))
    d["cosk"], d["sink"] = rope_tables(poskv, 1.0)
    k_i = np.arange(128)[:, None]
    q_i = np.arange(128)[None, :]
    if h == 0:
        m = np.where(k_i >= q_i, 0.0, NEG).astype(np.float32)
    else:
        m = np.where(q_i >= k_i, 0.0, NEG).astype(np.float32)
    d["maskc"] = m.astype(BF)
    return d


def kernel(L0_attn_hidden_states, L0_attn_Wq, L0_attn_Wk, L0_attn_Wv, L0_attn_Wo,
           L0_ffn_Wq, L0_ffn_Wr, L0_ffn_aux_keys, L0_ffn_aux_values, L0_ffn_Wvd,
           L0_ffn_Wvu, L0_ffn_Wu1, L0_ffn_bu1, L0_ffn_Wu2, L0_ffn_bu2,
           L0_ffn_gate_w1, L0_ffn_gate_bias, L0_ffn_log_rel):
    f = lambda a: np.asarray(a, dtype=np.float32)
    x = f(L0_attn_hidden_states)
    wd = dict(Wq=f(L0_attn_Wq), Wk=f(L0_attn_Wk), Wv=f(L0_attn_Wv), Wo=f(L0_attn_Wo),
              Wfq=f(L0_ffn_Wq), Wr=f(L0_ffn_Wr), ak=f(L0_ffn_aux_keys),
              avv=f(L0_ffn_aux_values), Wvd=f(L0_ffn_Wvd), Wvu=f(L0_ffn_Wvu),
              Wu1=f(L0_ffn_Wu1), bu1=f(L0_ffn_bu1), Wu2=f(L0_ffn_Wu2),
              bu2=f(L0_ffn_bu2), gw1=f(L0_ffn_gate_w1), gb=f(L0_ffn_gate_bias),
              lrel=f(L0_ffn_log_rel))
    shared = _prep_shared(wd)
    in_maps = []
    for c in range(8):
        b, h = c // 2, c % 2
        m = dict(shared)
        m.update(_prep_core(x, b, h))
        in_maps.append(m)
    if _NC_CACHE[0] is None:
        _NC_CACHE[0] = build_nc()
    nc = _NC_CACHE[0]
    res = run_bass_kernel_spmd(nc, in_maps, core_ids=list(range(8)))
    out = np.empty((B, S, E), dtype=np.float32)
    for c in range(8):
        b, h = c // 2, c % 2
        posq, _ = perm_positions(h)
        y = np.asarray(res.results[c]["y"]).reshape(E, 1024)
        out[b, posq, :] = y.T
    return out


# version 10
# speedup vs baseline: 1.0589x; 1.0589x over previous
import math
import numpy as np
import ml_dtypes
from contextlib import ExitStack

import concourse.bass as bass
import concourse.mybir as mybir
from concourse.bass_utils import run_bass_kernel_spmd

F32 = mybir.dt.float32
BF16 = mybir.dt.bfloat16
U8 = mybir.dt.uint8
ACT = mybir.ActivationFunctionType
ALU = mybir.AluOpType
AX = mybir.AxisListType.X
BF = ml_dtypes.bfloat16

B, S, E = 4, 2048, 4096
H, KVH, D = 32, 8, 128
TQ, TH = 1024, 512
EC, JT, UC = 32, 8, 8
NEG = -1.0e30
PT_OFF = [0, 128, 384, 768, 1280, 1920, 2688, 3584]
PT_TOT = 4608
DEBUG = False
SAFE_HEADS = False
SAFE_STAGES = False
COLL = True
PIPE_AB = True


def tile_w(WT):
    K, M = WT.shape
    Kc, Mc = K // 128, M // 128
    return np.ascontiguousarray(
        WT.reshape(Kc, 128, Mc, 128).transpose(2, 1, 0, 3).reshape(Mc, 128, K))


def tile_w64(WT):
    K, M = WT.shape
    Kc = K // 128
    return np.ascontiguousarray(WT.reshape(Kc, 128, M).transpose(1, 0, 2).reshape(128, Kc * M))


def rope_tables(positions, scale):
    inv_freq = 1.0 / (10000.0 ** (np.arange(0, D, 2, dtype=np.float64) / D))
    freqs = positions[:, None].astype(np.float64) * inv_freq[None, :]
    emb = np.concatenate([freqs, freqs], axis=-1)
    cos = (np.cos(emb).T * scale).astype(np.float32)
    sin = (np.sin(emb).T * scale).astype(np.float32)
    sin[:64] *= -1.0
    return np.ascontiguousarray(cos), np.ascontiguousarray(sin)


def perm_positions(h):
    if h == 0:
        return np.arange(0, 1024), np.arange(1024, 2048)
    q = (np.arange(1024, 2048).reshape(8, 128))[::-1].reshape(-1)
    kv = (np.arange(0, 1024).reshape(8, 128))[::-1].reshape(-1)
    return q, kv


class Sem:
    __slots__ = ("name", "n", "h")

    def __init__(self, name):
        self.name, self.n, self.h = name, 0, None


class Bank:
    __slots__ = ("idx", "guard")

    def __init__(self, idx):
        self.idx, self.guard = idx, None


class Plan:
    def __init__(self):
        self.ops = []
        self.sems = []
        self.tS = None
        self.hopA = None
        self.hopB = None
        self._hops = {}

    def sem(self, name):
        s = Sem(name)
        self.sems.append(s)
        return s

    def hop(self, v):
        if v in self._hops:
            return self._hops[v]
        tS, hB = self.tS, self.hopB
        assert v <= tS.n
        self.op("sync", lambda e, v=v: e.wait_ge(tS.h, v))
        self.op("sync", lambda e: e.nop(cycle_cnt=400), incs=[(hB, 1)])
        r = (hB, hB.n)
        self._hops[v] = r
        return r

    def op(self, eng, fn, waits=(), incs=()):
        ws = []
        for s, v in waits:
            assert v <= s.n, f"plan deadlock on {s.name}: wait {v} > planned {s.n}"
            if v > 0:
                if s is self.tS and eng in ("vector", "scalar"):
                    ws.append(self.hop(v))
                else:
                    ws.append((s, v))
        ics = []
        for s, a in incs:
            s.n += a
            ics.append((s, a))
        self.ops.append((eng, fn, ws, ics))


def build_nc():
    nc = bass.Bass("TRN2", target_bir_lowering=False)

    dri = lambda n, sh, dt=F32: nc.dram_tensor(n, sh, dt, kind="ExternalInput").ap()
    xq_h = dri("xq", [EC, 128, 1024])
    xkv_h = dri("xkv", [EC, 128, 1024])
    wkv_h = dri("wkv", [16, 128, E])
    wq_h = dri("wq", [32, 128, E])
    wo_h = dri("wo", [32, 128, E])
    wu1_h = dri("wu1", [8, 128, E])
    wfq_h = dri("wfq", [128, 2048])
    wvd_h = dri("wvd", [2, 128, E])
    wvu_h = dri("wvu", [32, 128, 256])
    wr_h = dri("wr", [64, 48])
    akt_h = dri("akt", [64, 1024])
    av_h = dri("av", [32, 128, 1024])
    w2r_h = dri("w2r", [8, 128, 128])
    bu1_h = dri("bu1c", [128, 8])
    lrel_h = dri("lrel", [2, 1024])
    bu2_h = dri("bu2r", [128, 1])
    gw1_h = dri("gw1r", [128, 1])
    gb_h = dri("gbr", [128, 1])
    cosq_h = dri("cosq", [128, 1024])
    sinq_h = dri("sinq", [128, 1024])
    cosk_h = dri("cosk", [128, 1024])
    sink_h = dri("sink", [128, 1024])
    eye_h = dri("eyec", [128, 128], BF16)
    ones_h = dri("onesc", [128, 128], BF16)
    p64_h = dri("p64c", [128, 128], BF16)
    mask_h = dri("maskc", [128, 128], BF16)
    ones1_h = dri("ones1c", [1, 512], BF16)

    y_h = nc.dram_tensor("y", [EC, 128, 1024], F32, kind="ExternalOutput").ap()
    dsm_h = nc.dram_tensor("dsm", [128, 16], F32, kind="ExternalOutput").ap()
    spill_r1 = nc.dram_tensor("spill_r1", [EC, 128, TH], F32, kind="Internal").ap()
    spill_h1 = nc.dram_tensor("spill_h1", [EC, 128, TH], F32, kind="Internal").ap()
    cc_in_t = nc.dram_tensor("cc_in", [128, 1], F32, kind="Internal")
    cc_out_t = nc.dram_tensor("cc_out", [128, 1], F32, kind="Internal")
    taps = {}
    if DEBUG:
        for name, sh in [("t_attn", [128, 32768]), ("t_qrope", [128, 32768]),
                         ("t_krope", [128, 8192]), ("t_vtm", [128, 8192]),
                         ("t_negc", [128, 32]), ("t_wun", [128, 1024]),
                         ("t_bot", [2, 128, 1024]), ("t_gate", [2, 128, 512]),
                         ("t_sums", [128, 1024])]:
            dt = F32 if name in ("t_negc", "t_gate", "t_sums") else BF16
            taps[name] = nc.dram_tensor(name, sh, dt, kind="ExternalOutput").ap()
        for name in ("t_r1", "t_h1", "t_r2"):
            taps[name] = nc.dram_tensor(name, [2, 128, 16384], F32, kind="ExternalOutput").ap()
        taps["t_pt0"] = nc.dram_tensor("t_pt0", [128, 4608], BF16, kind="ExternalOutput").ap()
        taps["t_sums0"] = nc.dram_tensor("t_sums0", [128, 2048], F32, kind="ExternalOutput").ap()
        taps["t_negc2"] = nc.dram_tensor("t_negc2", [128, 32], F32, kind="ExternalOutput").ap()
        taps["t_dbg"] = nc.dram_tensor("t_dbg", [128, 16], F32, kind="ExternalOutput").ap()
        taps["t_s1dbg"] = nc.dram_tensor("t_s1dbg", [128, 1024], F32, kind="ExternalOutput").ap()

    es = ExitStack()
    sb = lambda n, sh, dt: es.enter_context(nc.sbuf_tensor(n, sh, dt))
    ar1 = sb("ar1", [128, 32768], BF16)
    ar2 = sb("ar2", [128, 32768], BF16)
    ar2f = ar2[:].bitcast(F32)
    wbuf = sb("wbuf", [128, 8192], BF16)
    krope = sb("krope", [128, 8192], BF16)
    krf = krope[:].bitcast(F32)
    vtm = sb("vtm", [128, 8192], BF16)
    vtf = vtm[:].bitcast(F32)
    pt = sb("pt", [128, 9216], BF16)
    ptf = pt[:].bitcast(F32)
    ptu8 = pt[:].bitcast(U8)
    cs = sb("cs", [128, 2048], F32)
    csb = cs[:].bitcast(BF16)
    smalls = sb("smalls", [128, 16], F32)
    bu1_sb = sb("bu1sb", [128, 8], F32)
    t8_sb = sb("t8_sb", [128, 8], F32)
    eye = sb("eye", [128, 128], BF16)
    ones = sb("ones", [128, 128], BF16)
    p64 = sb("p64", [128, 128], BF16)
    maskt = sb("maskt", [128, 128], BF16)
    wr_sb = sb("wrsb", [64, 48], BF16)
    akaug = sb("akaug", [66, 1024], BF16)
    rkaug = sb("rkaug", [50, 1024], BF16)

    VTMP, KQSB = 0, 1024
    SCR1f, SCR2f = 1024, 2048
    SQBND = 6144
    SUMSf, RECIPf = 2304, 3328
    XQf, RBF, SQ2 = 0, 2048, 3072
    GSTGf = 2304
    MT, WFQ, W2R, GBF = 0, 4096, 6144, 7168
    MSKu8 = 16384
    rs_sb = vtf[:, 0:1024]
    a_sb = vtf[:, 1024:2048]
    wn_bf = vtm[:].bitcast(BF16)[:, 2048:3072]
    wun_sb = vtm[:].bitcast(BF16)[:, 0:1024]
    GATE = vtf[:, 2048:2560]
    M2 = vtf[:, 2560:3072]
    RSTD2 = vtf[:, 3072:3584]
    SCR = vtf[:, 3584:4096]
    msk_sb = ptu8[:, MSKu8:MSKu8 + 1024]
    def stats_sl(name, t):
        offs = {"m1": 0, "rstd1": 1024, "logvar": 2048, "sig": 3072}
        o = offs[name] + t * 512
        return krf[:, o:o + 512]
    botbf = csb[:, 0:2048]
    qfaug = csb[0:66, 2048:2560]
    rqaug = csb[0:50, 2560:3072]
    wbft = wbuf

    (SM_BU2, SM_GW1, SM_GB, SM_ZERO, SM_NEG, SM_NM, SM_SUM8, SM_REC8,
     SM_GM, SM_NVS, SM_T0, SM_T1) = range(12)
    SM_EPS, SM_C18, SM_CRQ, SM_N40 = 12, 13, 14, 15

    p = [es.enter_context(nc.psum_tensor(f"pb{i}", [128, 512], F32)) for i in range(8)]
    pbf = [pi[:].bitcast(BF16) for pi in p]
    banks = [Bank(i) for i in range(8)]

    pl = Plan()
    dW = pl.sem("dW")
    dX = pl.sem("dX")
    dO = pl.sem("dO")
    tS = pl.sem("tS")
    vS = pl.sem("vS")
    aS = pl.sem("aS")
    hA = pl.sem("hA")
    hB = pl.sem("hB")
    cS = pl.sem("cS")
    pl.tS, pl.hopA, pl.hopB = tS, hA, hB

    def stage_bar():
        if not SAFE_STAGES:
            return
        bar = [(tS, tS.n), (vS, vS.n), (aS, aS.n), (dW, dW.n), (dX, dX.n)]
        for en in ("tensor", "vector", "scalar", "gpsimd", "sync"):
            pl.op(en, None, waits=bar)

    def dma(q, out_ap, in_ap, sem, waits=()):
        pl.op(q, lambda e, o=out_ap, i=in_ap: e.dma_start(o, i), waits=waits, incs=[(sem, 16)])

    def bank_w(blist, waits=()):
        ws = list(waits)
        for b in blist:
            if b.guard is not None:
                ws.append(b.guard)
                b.guard = None
        return ws

    tap_guard = [None]

    def tap(name, src_ap, idx=None, waits=()):
        if not DEBUG:
            return
        dst = taps[name] if idx is None else taps[name][idx]
        dma("sync", dst, src_ap, dO, waits=waits)
        tap_guard[0] = (dO, dO.n)
        for en in ("tensor", "vector", "scalar", "gpsimd", "sync"):
            pl.op(en, None, waits=[(dO, dO.n)])

    def pop_tap():
        if tap_guard[0] is None:
            return []
        g = [tap_guard[0]]
        tap_guard[0] = None
        return g

    dma("sync", eye[:], eye_h, dX)
    dma("sync", ones[:], ones_h, dX)
    dma("sync", p64[:], p64_h, dX)
    dma("sync", maskt[:], mask_h, dX)
    dma("sync", cs[:, 0:1024], cosk_h, dX)
    dma("sync", cs[:, 1024:2048], sink_h, dX)
    dma("sync", bu1_sb[:], bu1_h, dX)
    dma("sync", smalls[:, SM_BU2:SM_BU2 + 1], bu2_h, dX)
    dma("sync", smalls[:, SM_GW1:SM_GW1 + 1], gw1_h, dX)
    dma("sync", smalls[:, SM_GB:SM_GB + 1], gb_h, dX)
    dma("gpsimd", akaug[0:64, :], akt_h, dW)
    dma("gpsimd", akaug[64:66, :], lrel_h, dW)
    dma("gpsimd", rkaug[48:50, :], lrel_h, dW)
    dma("gpsimd", wr_sb[:], wr_h, dW)
    pl.op("vector", lambda e: e.memset(smalls[:, SM_ZERO:SM_ZERO + 1], 0.0), incs=[(vS, 1)])
    pl.op("vector", lambda e: e.memset(smalls[:, SM_NEG:SM_NEG + 1], NEG), incs=[(vS, 1)])
    pl.op("vector", lambda e: e.memset(smalls[:, SM_EPS:SM_EPS + 1], 1e-5), incs=[(vS, 1)])
    pl.op("vector", lambda e: e.memset(smalls[:, SM_C18:SM_C18 + 1], 0.125), incs=[(vS, 1)])
    pl.op("vector", lambda e: e.memset(smalls[:, SM_CRQ:SM_CRQ + 1], 8.0 / math.sqrt(48.0)),
          incs=[(vS, 1)])
    pl.op("vector", lambda e: e.memset(smalls[:, SM_N40:SM_N40 + 1], -40.0), incs=[(vS, 1)])
    for fn_ in (ACT.Exp, ACT.Ln, ACT.Sqrt, ACT.Sigmoid, ACT.Gelu_apprx_tanh, ACT.Copy):
        pl.op("scalar", lambda e, fn_=fn_: e.activation(smalls[:, SM_NM:SM_NM + 1],
                                                        smalls[:, SM_ZERO:SM_ZERO + 1], fn_),
              waits=[(vS, vS.n)], incs=[(aS, 1)])
    consts_dX = dX.n
    consts_dW = dW.n

    wslot_guard = [None, None]

    def load_w(slot, in_ap, n_el=4096):
        ws = [] if wslot_guard[slot] is None else [wslot_guard[slot]]
        wslot_guard[slot] = None
        dma("gpsimd", wbuf[:, slot * 4096: slot * 4096 + n_el], in_ap, dW, waits=ws)
        return dW.n

    def proj_mm(bA, bB, wslot, wait_dw, x_dw, extra=()):
        ws = bank_w([bA, bB], waits=[(dW, wait_dw), (dW, x_dw)] + list(extra))
        for fc in range(EC):
            for hf in range(2):
                tgt = p[bA.idx] if hf == 0 else p[bB.idx]

                def mm(e, fc=fc, hf=hf, tgt=tgt, wslot=wslot):
                    return e.matmul(
                        tgt[:],
                        wbuf[:, wslot * 4096 + fc * 128: wslot * 4096 + (fc + 1) * 128],
                        ar1[:, fc * 1024 + hf * 512: fc * 1024 + (hf + 1) * 512],
                        start=(fc == 0), stop=(fc == EC - 1))
                pl.op("tensor", mm, waits=(ws if (fc == 0 and hf == 0) else ()),
                      incs=([(tS, 1)] if (fc == EC - 1 and hf == 1) else ()))
        return tS.n

    kq_guard = [None, None]
    KQOFF = [KQSB, VTMP]

    def rope_copy(pr, buf):
        kqoff = KQOFF[buf]
        ws0 = [(tS, tS.n)] + ([kq_guard[buf]] if kq_guard[buf] else [])
        kq_guard[buf] = None
        pl.op("scalar", lambda e, kqoff=kqoff, pr=pr:
              e.copy(pt[:, kqoff:kqoff + 512], p[pr[0].idx][:]),
              waits=ws0, incs=[(aS, 1)])
        pl.op("scalar", lambda e, kqoff=kqoff, pr=pr:
              e.copy(pt[:, kqoff + 512:kqoff + 1024], p[pr[1].idx][:]),
              incs=[(aS, 1)])
        return aS.n

    def rope_finish(pr, permp, buf, a_copy, out_buf, out_off):
        kqoff = KQOFF[buf]
        wsp = bank_w(list(permp), waits=[(aS, a_copy), (dX, consts_dX)])
        pl.op("tensor", lambda e, kqoff=kqoff, permp=permp:
              e.matmul(p[permp[0].idx][:], p64[:], pt[:, kqoff:kqoff + 512],
                       start=True, stop=True), waits=wsp)
        pl.op("tensor", lambda e, kqoff=kqoff, permp=permp:
              e.matmul(p[permp[1].idx][:], p64[:], pt[:, kqoff + 512:kqoff + 1024],
                       start=True, stop=True), incs=[(tS, 1)])
        t_perm = tS.n
        kq_guard[buf] = (tS, t_perm)
        for hf in range(2):
            pl.op("vector", lambda e, hf=hf, pr=pr:
                  e.tensor_mul(ptf[:, SCR1f + hf * 512: SCR1f + (hf + 1) * 512],
                               p[pr[hf].idx][:], cs[:, hf * 512:(hf + 1) * 512]),
                  waits=([(tS, t_perm)] if hf == 0 else ()), incs=[(vS, 1)])
        for hf in range(2):
            pl.op("vector", lambda e, hf=hf, permp=permp:
                  e.tensor_mul(ptf[:, SCR2f + hf * 512: SCR2f + (hf + 1) * 512],
                               p[permp[hf].idx][:], cs[:, 1024 + hf * 512: 1024 + (hf + 1) * 512]),
                  incs=[(vS, 1)])
        for hf in range(2):
            pl.op("vector", lambda e, hf=hf:
                  e.tensor_add(out_buf[:, out_off + hf * 512: out_off + (hf + 1) * 512],
                               ptf[:, SCR1f + hf * 512: SCR1f + (hf + 1) * 512],
                               ptf[:, SCR2f + hf * 512: SCR2f + (hf + 1) * 512]),
                  incs=[(vS, 1)])
        v_done = vS.n
        for b in tuple(pr) + tuple(permp):
            b.guard = (vS, v_done)
        return v_done

    for i in range(4):
        dma("gpsimd", ar1[:, i * 8192:(i + 1) * 8192],
            xkv_h[i * 8:(i + 1) * 8].transpose([1, 0, 2]), dW)
    x_dw, x_dx = dW.n, dX.n

    kv_order = []
    for g in range(8):
        kv_order += [g, 8 + g]
    wready = {}
    for idx in range(2):
        wready[kv_order[idx]] = load_w(idx % 2, wkv_h[kv_order[idx]])

    def v_evac(g, a_v):
        for hf in range(2):
            b6 = banks[6 + hf]
            wsb = bank_w([b6], waits=[(aS, a_v), (dX, consts_dX)])
            for j in range(4):
                def tr(e, hf=hf, j=j, b6=b6):
                    return e.transpose(pbf[b6.idx][:, j * 128:(j + 1) * 128],
                                       pt[:, VTMP + hf * 512 + j * 128:
                                          VTMP + hf * 512 + (j + 1) * 128],
                                       eye[:])
                pl.op("tensor", tr, waits=(wsb if j == 0 else ()),
                      incs=([(tS, 1)] if j == 3 else ()))
            if hf == 1:
                kq_guard[1] = (tS, tS.n)
            t_tr = tS.n
            dst_off = (g * 8 + hf * 4) * 128
            pl.op("vector", lambda e, dst_off=dst_off, b6=b6:
                  e.tensor_copy(vtm[:, dst_off:dst_off + 512], pbf[b6.idx][:, 0:512]),
                  waits=[(tS, t_tr)], incs=[(vS, 1)])
            b6.guard = (vS, vS.n)

    pendV = None
    for g in range(8):
        proj_mm(banks[0], banks[1], 0, wready[g], x_dw, extra=[(dX, x_dx)])
        wslot_guard[0] = (tS, tS.n)
        if 2 * g + 2 < 16:
            wready[kv_order[2 * g + 2]] = load_w(0, wkv_h[kv_order[2 * g + 2]])
        a_k = rope_copy((banks[0], banks[1]), 0)
        if pendV is not None and PIPE_AB:
            v_evac(*pendV)
        rope_finish((banks[0], banks[1]), (banks[4], banks[5]), 0, a_k,
                    krope, g * 1024)
        proj_mm(banks[2], banks[3], 1, wready[8 + g], x_dw, extra=[(dX, x_dx)])
        wslot_guard[1] = (tS, tS.n)
        if 2 * g + 3 < 16:
            wready[kv_order[2 * g + 3]] = load_w(1, wkv_h[kv_order[2 * g + 3]])
        t_v = tS.n
        ws = [(tS, t_v)] + ([kq_guard[1]] if kq_guard[1] else [])
        kq_guard[1] = None
        pl.op("scalar", lambda e: e.copy(pt[:, VTMP:VTMP + 512], p[banks[2].idx][:]),
              waits=ws, incs=[(aS, 1)])
        pl.op("scalar", lambda e: e.copy(pt[:, VTMP + 512:VTMP + 1024], p[banks[3].idx][:]),
              incs=[(aS, 1)])
        banks[2].guard = (aS, aS.n)
        banks[3].guard = (aS, aS.n)
        a_v = aS.n
        pendV = (g, a_v)
        if not PIPE_AB:
            v_evac(*pendV)
            pendV = None
    if pendV is not None:
        v_evac(*pendV)

    cs_last_read = vS.n
    dma("sync", cs[:, 0:1024], cosq_h, dX, waits=[(vS, cs_last_read)])
    dma("sync", cs[:, 1024:2048], sinq_h, dX)
    csq_dx = dX.n
    tap("t_krope", krope[:], waits=[(vS, vS.n)])
    tap("t_vtm", vtm[:], waits=[(vS, vS.n)])

    t_a_done = tS.n
    for i in range(4):
        dma("gpsimd", ar1[:, i * 8192:(i + 1) * 8192],
            xq_h[i * 8:(i + 1) * 8].transpose([1, 0, 2]), dW,
            waits=([(tS, t_a_done)] if i == 0 else ()))
    x_dw, x_dx = dW.n, dX.n

    PAIR3 = [(banks[0], banks[1]), (banks[2], banks[3]), (banks[4], banks[5])]
    wready = {0: load_w(0, wq_h[0]), 1: load_w(1, wq_h[1])}
    pendQ = None
    for h in range(H):
        pr = PAIR3[h % 3]
        extra = ([(dX, csq_dx)] if h == 0 else []) + [(dX, x_dx)]
        proj_mm(pr[0], pr[1], h % 2, wready[h], x_dw, extra=extra)
        wslot_guard[h % 2] = (tS, tS.n)
        if h + 2 < H:
            wready[h + 2] = load_w(h % 2, wq_h[h + 2])
        a_copy = rope_copy(pr, h % 2)
        if pendQ is not None and PIPE_AB:
            rope_finish(*pendQ)
        pendQ = (pr, (banks[6], banks[7]), h % 2, a_copy, ar2, h * 1024)
        if not PIPE_AB:
            rope_finish(*pendQ)
            pendQ = None
    if pendQ is not None:
        rope_finish(*pendQ)
    tap("t_qrope", ar2[:], waits=[(vS, vS.n)])
    qrope_v = vS.n

    pt_guard = [wslot_guard[0], wslot_guard[1]]
    wslot_guard[0] = wslot_guard[1] = None
    sums_guard = [None]
    for h in range(H):
        g = h // 4
        buf = h % 2
        ptb = pt if buf == 0 else wbuf

        def qk_tiles(with_mask, exp_bias, exp_scale, first_waits):
            exp_marks = []
            ws_head = list(first_waits)
            for J in range(JT):
                span = (J + 1) * 128
                sbp = (banks[0], banks[1]) if J % 2 == 0 else (banks[2], banks[3])
                n0 = min(span, 512)
                ws = bank_w(list(sbp), waits=ws_head)
                ws_head = []
                pl.op("tensor", lambda e, g=g, J=J, n0=n0, b=sbp[0], h=h:
                      e.matmul(p[b.idx][:, 0:n0],
                               krope[:, g * 1024 + J * 128: g * 1024 + (J + 1) * 128],
                               ar2[:, h * 1024: h * 1024 + n0],
                               start=True, stop=(span <= 512 and not with_mask)),
                      waits=ws)
                if span > 512:
                    pl.op("tensor", lambda e, g=g, J=J, span=span, b=sbp[1], h=h:
                          e.matmul(p[b.idx][:, 0:span - 512],
                                   krope[:, g * 1024 + J * 128: g * 1024 + (J + 1) * 128],
                                   ar2[:, h * 1024 + 512: h * 1024 + span],
                                   start=True, stop=(not with_mask)))
                if with_mask:
                    dcol = span - 128
                    if dcol < 512:
                        tgt, off = sbp[0], dcol
                    else:
                        tgt, off = sbp[1], dcol - 512
                    pl.op("tensor", lambda e, tgt=tgt, off=off:
                          e.matmul(p[tgt.idx][:, off:off + 128], eye[:], maskt[:],
                                   start=False, stop=True))
                pl.op("tensor", lambda e: e.engine_nop() if False else None, waits=()) \
                    if False else None
                pl.ops[-1][3].append((tS, 0)) if False else None
                eng, fn, wss, ics = pl.ops[-1]
                tS.n += 1
                ics.append((tS, 1))
                t_sc = tS.n
                ws_exp = [(tS, t_sc)]
                if exp_marks == [] and pt_guard[buf] is not None:
                    ws_exp.append(pt_guard[buf])
                    pt_guard[buf] = None
                pl.op("scalar", lambda e, b=sbp[0], n0=n0, J=J, ptb=ptb, eb=exp_bias, sc=exp_scale:
                      e.activation(ptb[:, PT_OFF[J]: PT_OFF[J] + n0],
                                   p[b.idx][:, 0:n0], ACT.Exp, bias=eb, scale=sc),
                      waits=ws_exp, incs=[(aS, 1)])
                if span > 512:
                    pl.op("scalar", lambda e, b=sbp[1], span=span, J=J, ptb=ptb, eb=exp_bias, sc=exp_scale:
                          e.activation(ptb[:, PT_OFF[J] + 512: PT_OFF[J] + span],
                                       p[b.idx][:, 0:span - 512], ACT.Exp, bias=eb, scale=sc),
                          incs=[(aS, 1)])
                exp_marks.append(aS.n)
                sbp[0].guard = (aS, aS.n)
                sbp[1].guard = (aS, aS.n)
            return exp_marks

        def sums_mm(exp_marks):
            ws6 = bank_w([banks[6]], waits=[(aS, exp_marks[0])])
            ws7 = bank_w([banks[7]])
            for J in range(JT):
                span = (J + 1) * 128
                n0 = min(span, 512)
                pl.op("tensor", lambda e, J=J, n0=n0, ptb=ptb:
                      e.matmul(p[6][:, 0:n0], ones[:], ptb[:, PT_OFF[J]: PT_OFF[J] + n0],
                               start=(J == 0), stop=(J == JT - 1)),
                      waits=(ws6 if J == 0 else [(aS, exp_marks[J])]),
                      incs=([(tS, 1)] if J == JT - 1 else ()))
                if span > 512:
                    pl.op("tensor", lambda e, J=J, span=span, ptb=ptb:
                          e.matmul(p[7][:, 0:span - 512], ones[:],
                                   ptb[:, PT_OFF[J] + 512: PT_OFF[J] + span],
                                   start=(J == 4), stop=(J == JT - 1)),
                          waits=(ws7 if J == 4 else ()),
                          incs=([(tS, 1)] if J == JT - 1 else ()))
            return tS.n

        stage_bar()
        m2m = qk_tiles(True, smalls[:, SM_ZERO:SM_ZERO + 1], 1.0, [(vS, qrope_v)])
        stage_bar()
        t_s2 = sums_mm(m2m)
        stage_bar()
        ws = [(tS, t_s2)] + ([sums_guard[0]] if sums_guard[0] else [])
        sums_guard[0] = None
        pl.op("vector", lambda e: e.reciprocal(ptf[:, RECIPf:RECIPf + 512], p[6][:]),
              waits=ws, incs=[(vS, 1)])
        pl.op("vector", lambda e: e.reciprocal(ptf[:, RECIPf + 512:RECIPf + 1024], p[7][:]),
              incs=[(vS, 1)])
        banks[6].guard = (vS, vS.n)
        banks[7].guard = (vS, vS.n)
        recip_v = vS.n
        for qh in range(2):
            bav = banks[4 + qh]
            Js = list(range(JT)) if qh == 0 else list(range(4, JT))
            ws = bank_w([bav], waits=[(aS, m2m[-1])])
            for J in Js:
                span = (J + 1) * 128
                if qh == 0:
                    w = min(span, 512)
                    src_off = PT_OFF[J]
                else:
                    w = span - 512
                    src_off = PT_OFF[J] + 512
                pl.op("tensor", lambda e, g=g, J=J, w=w, src_off=src_off, bav=bav, ptb=ptb,
                      st=(J == Js[0]):
                      e.matmul(p[bav.idx][:, 0:w],
                               vtm[:, (g * 8 + J) * 128: (g * 8 + J + 1) * 128],
                               ptb[:, src_off: src_off + w],
                               start=st, stop=(J == JT - 1)),
                      waits=(ws if J == Js[0] else ()),
                      incs=([(tS, 1)] if J == JT - 1 else ()))
            t_av = tS.n
            pl.op("vector", lambda e, h=h, qh=qh, bav=bav:
                  e.tensor_mul(ar1[:, h * 1024 + qh * 512: h * 1024 + (qh + 1) * 512],
                               p[bav.idx][:], ptf[:, RECIPf + qh * 512: RECIPf + (qh + 1) * 512]),
                  waits=[(tS, t_av), (vS, recip_v)], incs=[(vS, 1)])
            bav.guard = (vS, vS.n)
        sums_guard[0] = (vS, vS.n)
        pt_guard[buf] = (tS, tS.n)
        if DEBUG and h == 0:
            tap("t_pt0", ptb[:, 0:4608], waits=[(vS, vS.n)])
            tap("t_sums0", ptf[:, 2304:4352], waits=[(vS, vS.n)])
        stage_bar()
        if SAFE_HEADS:
            bar = [(tS, tS.n), (vS, vS.n), (aS, aS.n)]
            for en in ("tensor", "vector", "scalar"):
                pl.op(en, None, waits=bar)
    attn_done_v = vS.n
    wslot_guard[0] = pt_guard[1]
    wslot_guard[1] = pt_guard[1]
    tap("t_attn", ar1[:], waits=[(vS, attn_done_v)])

    xq_guard = [None, None]
    rbf_guard = [None, None]
    mt_guard = [None]
    d_stats_t = [0]

    def phase_D(t, spill, first_waits=()):
        wready2 = {0: load_w(0, wo_h[0]), 1: load_w(1, wo_h[1])}
        xq_dx = {}
        for ec in range(2):
            ws = list(first_waits) if ec == 0 else []
            ws += [xq_guard[ec % 2]] if xq_guard[ec % 2] else []
            dma("sync", ptf[:, XQf + (ec % 2) * 512: XQf + (ec % 2 + 1) * 512],
                xq_h[ec, :, t * 512:(t + 1) * 512], dX, waits=ws)
            xq_dx[ec] = dX.n
        first_rbf = list(first_waits)
        RBF4, SQ4 = 2048, 4096
        v_sq_group = [0] * 8
        rbf4_guard = [None] * 4

        def stats_group(g):
            ws6 = bank_w([banks[6]], waits=[(vS, v_sq_group[g])])
            ws7 = bank_w([banks[7]])
            for j in range(4):
                ec = g * 4 + j
                pl.op("tensor", lambda e, j=j, ec=ec:
                      e.matmul(p[6][:], ones[:], pt[:, RBF4 + j * 512: RBF4 + (j + 1) * 512],
                               start=(ec == 0), stop=(ec == EC - 1)),
                      waits=(ws6 if j == 0 else []))
                pl.op("tensor", lambda e, j=j, ec=ec:
                      e.matmul(p[7][:], ones[:], pt[:, SQ4 + j * 512: SQ4 + (j + 1) * 512],
                               start=(ec == 0), stop=(ec == EC - 1)),
                      waits=(ws7 if j == 0 else []),
                      incs=([(tS, 1)] if j == 3 else ()))
            for j in range(4):
                rbf4_guard[j] = (tS, tS.n)

        for ec in range(EC):
            bwo = banks[ec % 4]
            ws = bank_w([bwo], waits=[(dW, wready2[ec]), (vS, attn_done_v)] +
                        (list(first_waits) if ec == 0 else []))
            for hd in range(H):
                def mm(e, hd=hd, ec=ec, t=t, wslot=ec % 2, bwo=bwo):
                    return e.matmul(
                        p[bwo.idx][:],
                        wbuf[:, wslot * 4096 + hd * 128: wslot * 4096 + (hd + 1) * 128],
                        ar1[:, hd * 1024 + t * 512: hd * 1024 + (t + 1) * 512],
                        start=(hd == 0), stop=(hd == H - 1))
                pl.op("tensor", mm, waits=(ws if hd == 0 else ()),
                      incs=([(tS, 1)] if hd == H - 1 else ()))
            wslot_guard[ec % 2] = (tS, tS.n)
            if ec + 2 < EC:
                wready2[ec + 2] = load_w(ec % 2, wo_h[ec + 2])
            if ec % 4 == 0 and ec >= 4:
                stats_group(ec // 4 - 1)
            t_wo = tS.n
            pl.op("vector", lambda e, ec=ec, bwo=bwo:
                  e.tensor_add(ar2f[:, ec * 512:(ec + 1) * 512], p[bwo.idx][:],
                               ptf[:, XQf + (ec % 2) * 512: XQf + (ec % 2 + 1) * 512]),
                  waits=[(tS, t_wo), (dX, xq_dx[ec])], incs=[(vS, 1)])
            bwo.guard = (vS, vS.n)
            xq_guard[ec % 2] = (vS, vS.n)
            if ec + 2 < EC:
                dma("sync", ptf[:, XQf + (ec % 2) * 512: XQf + (ec % 2 + 1) * 512],
                    xq_h[ec + 2, :, t * 512:(t + 1) * 512], dX, waits=[xq_guard[ec % 2]])
                xq_dx[ec + 2] = dX.n
            v_r1 = vS.n
            ws = list(first_rbf)
            first_rbf = []
            ws += [rbf4_guard[ec % 4]] if rbf4_guard[ec % 4] else []
            pl.op("vector", lambda e, ec=ec:
                  e.tensor_copy(pt[:, RBF4 + (ec % 4) * 512: RBF4 + (ec % 4 + 1) * 512],
                                ar2f[:, ec * 512:(ec + 1) * 512]),
                  waits=ws, incs=[(vS, 1)])
            rbf4_guard[ec % 4] = None
            pl.op("vector", lambda e, ec=ec:
                  e.tensor_mul(pt[:, SQ4 + (ec % 4) * 512: SQ4 + (ec % 4 + 1) * 512],
                               ar2f[:, ec * 512:(ec + 1) * 512],
                               ar2f[:, ec * 512:(ec + 1) * 512]),
                  incs=[(vS, 1)])
            v_sq_group[ec // 4] = vS.n
            if spill:
                dma("sync", spill_r1[ec], ar2f[:, ec * 512:(ec + 1) * 512], dO,
                    waits=[(vS, v_r1)])
            stage_bar()
        stats_group(7)
        d_stats_t[0] = tS.n
        t_stats = tS.n
        pl.op("vector", lambda e, t=t: e.tensor_scalar_mul(stats_sl("m1", t), p[6][:], 1.0 / E),
              waits=[(tS, t_stats)], incs=[(vS, 1)])
        pl.op("vector", lambda e: e.tensor_scalar_mul(SCR, p[7][:], 1.0 / E), incs=[(vS, 1)])
        banks[6].guard = (vS, vS.n)
        banks[7].guard = (vS, vS.n)
        pl.op("vector", lambda e, t=t: e.tensor_mul(RSTD2, stats_sl("m1", t), stats_sl("m1", t)),
              waits=[(vS, vS.n)], incs=[(vS, 1)])
        pl.op("vector", lambda e: e.tensor_sub(SCR, SCR, RSTD2),
              waits=[(vS, vS.n)], incs=[(vS, 1)])
        pl.op("vector", lambda e, t=t: e.tensor_copy(stats_sl("logvar", t), SCR),
              waits=[(vS, vS.n)], incs=[(vS, 1)])
        pl.op("scalar", lambda e: e.activation(SCR, SCR, ACT.Sqrt,
                                               bias=smalls[:, SM_EPS:SM_EPS + 1]),
              waits=[(vS, vS.n)], incs=[(aS, 1)])
        pl.op("vector", lambda e, t=t: e.reciprocal(stats_sl("rstd1", t), SCR),
              waits=[(aS, aS.n)], incs=[(vS, 1)])
        pl.op("vector", lambda e, t=t: e.tensor_mul(RSTD2, stats_sl("rstd1", t),
                                                    stats_sl("rstd1", t)),
              waits=[(vS, vS.n)], incs=[(vS, 1)])
        pl.op("vector", lambda e, t=t: e.tensor_mul(RSTD2, RSTD2, stats_sl("logvar", t)),
              waits=[(vS, vS.n)], incs=[(vS, 1)])
        pl.op("scalar", lambda e, t=t: e.activation(stats_sl("logvar", t), RSTD2, ACT.Ln, bias=1.0),
              waits=[(vS, vS.n)], incs=[(aS, 1)])
        pl.op("vector", lambda e, t=t:
              e.reduce_sum(smalls[:, SM_T0 + t:SM_T0 + t + 1], stats_sl("logvar", t), axis=AX),
              waits=[(aS, aS.n)], incs=[(vS, 1)])
        return vS.n

    h1bf_written = [0, 0]

    def phase_E(t, first_waits):
        fw = list(first_waits)
        if mt_guard[0] is not None:
            fw.append(mt_guard[0])
            mt_guard[0] = None
        marks = []
        for ec in range(EC):
            ws = fw if ec == 0 else []
            pl.op("vector", lambda e, ec=ec, t=t:
                  e.tensor_sub(SCR, ar2f[:, ec * 512:(ec + 1) * 512], stats_sl("m1", t)),
                  waits=ws, incs=[(vS, 1)])
            pl.op("vector", lambda e, ec=ec, t=t:
                  e.tensor_mul(ar2f[:, ec * 512:(ec + 1) * 512], SCR, stats_sl("rstd1", t)),
                  incs=[(vS, 1)])
            pl.op("vector", lambda e, ec=ec:
                  e.tensor_copy(ar1[:, ec * 512:(ec + 1) * 512],
                                ar2f[:, ec * 512:(ec + 1) * 512]),
                  incs=[(vS, 1)])
            marks.append(vS.n)
        h1bf_written[t] = vS.n
        return vS.n, marks

    rk_done = [None]
    aug_dx = [None]
    wun_guard = [None]
    botbf_guard = [None]

    def preload_F():
        dma("gpsimd", pt[:, WFQ:WFQ + 2048], wfq_h, dW,
            waits=[(vS, attn_done_v), (tS, d_stats_t[0])])
        dma("gpsimd", pt[:, W2R:W2R + 1024], w2r_h.transpose([1, 0, 2]), dW)
        wfq_dw = dW.n
        avr = {0: load_w(0, av_h[0:4].transpose([1, 0, 2])),
               1: load_w(1, av_h[4:8].transpose([1, 0, 2]))}
        return wfq_dw, avr

    def phase_F(t, pre=None):
        h1v = h1bf_written[t]
        wfq_dw, avr_pre = pre if pre is not None else preload_F()
        if rk_done[0] is None:
            dma("sync", qfaug[64:65, :], ones1_h, dX, waits=[(vS, attn_done_v)])
            dma("sync", qfaug[65:66, :], ones1_h, dX)
            dma("sync", rqaug[48:49, :], ones1_h, dX)
            dma("sync", rqaug[49:50, :], ones1_h, dX)
            aug_dx[0] = dX.n
            ws = bank_w([banks[0], banks[1]], waits=[(dW, consts_dW)])
            pl.op("tensor", lambda e: e.matmul(p[0][0:48, :], wr_sb[:], akaug[0:64, 0:512],
                                               start=True, stop=True), waits=ws)
            pl.op("tensor", lambda e: e.matmul(p[1][0:48, :], wr_sb[:], akaug[0:64, 512:1024],
                                               start=True, stop=True), incs=[(tS, 1)])
            pl.op("scalar", lambda e: e.copy(rkaug[0:48, 0:512], p[0][0:48, :]),
                  waits=[(tS, tS.n)], incs=[(aS, 1)])
            pl.op("scalar", lambda e: e.copy(rkaug[0:48, 512:1024], p[1][0:48, :]),
                  incs=[(aS, 1)])
            banks[0].guard = (aS, aS.n)
            banks[1].guard = (aS, aS.n)
            rk_done[0] = (aS, aS.n)
        b4 = banks[4]
        ws = bank_w([b4], waits=[(dW, wfq_dw), (vS, h1v)])
        for fc in range(EC):
            pl.op("tensor", lambda e, fc=fc, b4=b4:
                  e.matmul(p[b4.idx][0:64, :], pt[:, WFQ + fc * 64: WFQ + (fc + 1) * 64],
                           ar1[:, fc * 512:(fc + 1) * 512],
                           start=(fc == 0), stop=(fc == EC - 1)),
                  waits=(ws if fc == 0 else ()), incs=([(tS, 1)] if fc == EC - 1 else ()))
        ws = [(tS, tS.n)] + ([botbf_guard[0]] if botbf_guard[0] else [])
        botbf_guard[0] = None
        pl.op("scalar", lambda e: e.activation(qfaug[0:64, :], p[b4.idx][0:64, :],
                                               ACT.Copy, scale=smalls[0:64, SM_C18:SM_C18 + 1]),
              waits=ws, incs=[(aS, 1)])
        b4.guard = (aS, aS.n)
        qf_as = aS.n
        b5 = banks[5]
        ws = bank_w([b5], waits=[(aS, qf_as)])
        pl.op("tensor", lambda e, b5=b5: e.matmul(p[b5.idx][0:48, :], wr_sb[:], qfaug[0:64, :],
                                                  start=True, stop=True),
              waits=ws, incs=[(tS, 1)])
        pl.op("scalar", lambda e, b5=b5: e.activation(rqaug[0:48, :], p[b5.idx][0:48, :],
                                                      ACT.Copy, scale=smalls[0:48, SM_CRQ:SM_CRQ + 1]),
              waits=[(tS, tS.n)], incs=[(aS, 1)])
        b5.guard = (aS, aS.n)
        rq_as = aS.n
        for tt in range(4):
            pr = (banks[0], banks[1]) if tt % 2 == 0 else (banks[2], banks[3])
            ws = bank_w(list(pr), waits=[(aS, rq_as), rk_done[0], (dX, aug_dx[0])])
            for hf in range(2):
                pl.op("tensor", lambda e, tt=tt, hf=hf, b=pr[hf]:
                      e.matmul(p[b.idx][:], rqaug[:, tt * 128:(tt + 1) * 128],
                               rkaug[:, hf * 512:(hf + 1) * 512], start=True, stop=True),
                      waits=(ws if hf == 0 else ()), incs=([(tS, 1)] if hf == 1 else ()))
            t_rs = tS.n
            ws_rs = [(tS, t_rs)] + ([wun_guard[0]] if wun_guard[0] else [])
            wun_guard[0] = None
            for hf in range(2):
                pl.op("scalar", lambda e, hf=hf, b=pr[hf]:
                      e.copy(rs_sb[:, hf * 512:(hf + 1) * 512], p[b.idx][:]),
                      waits=(ws_rs if hf == 0 else ()), incs=[(aS, 1)])
            rs_as = aS.n
            pr[0].guard = (aS, rs_as)
            pr[1].guard = (aS, rs_as)
            ws = bank_w(list(pr), waits=[(aS, qf_as), (dX, aug_dx[0])])
            for hf in range(2):
                pl.op("tensor", lambda e, tt=tt, hf=hf, b=pr[hf]:
                      e.matmul(p[b.idx][:], qfaug[:, tt * 128:(tt + 1) * 128],
                               akaug[:, hf * 512:(hf + 1) * 512], start=True, stop=True),
                      waits=(ws if hf == 0 else ()), incs=([(tS, 1)] if hf == 1 else ()))
            t_lg = tS.n
            for hf in range(2):
                pl.op("scalar", lambda e, hf=hf, b=pr[hf]:
                      e.copy(a_sb[:, hf * 512:(hf + 1) * 512], p[b.idx][:]),
                      waits=([(tS, t_lg)] if hf == 0 else ()), incs=[(aS, 1)])
            a_as = aS.n
            pr[0].guard = (aS, a_as)
            pr[1].guard = (aS, a_as)
            pl.op("vector", lambda e: e.max(out=t8_sb[:], in_=rs_sb), waits=[(aS, rs_as)],
                  incs=[(vS, 1)])
            pl.op("vector", lambda e: e.tensor_scalar(msk_sb, rs_sb, t8_sb[:, 7:8],
                                                      None, ALU.is_ge),
                  waits=[(vS, vS.n)], incs=[(vS, 1)])
            pl.op("vector", lambda e: e.select(rs_sb, msk_sb, a_sb,
                                               smalls[:, SM_NEG:SM_NEG + 1].to_broadcast([128, 1024])),
                  waits=[(aS, a_as), (vS, vS.n)], incs=[(vS, 1)])
            pl.op("scalar", lambda e: e.activation(wn_bf, rs_sb, ACT.Exp,
                                                   bias=smalls[:, SM_ZERO:SM_ZERO + 1],
                                                   accum_out=smalls[:, SM_SUM8:SM_SUM8 + 1]),
                  waits=[(vS, vS.n)], incs=[(aS, 1)])
            pl.op("vector", lambda e: e.reciprocal(smalls[:, SM_REC8:SM_REC8 + 1],
                                                   smalls[:, SM_SUM8:SM_SUM8 + 1]),
                  waits=[(aS, aS.n)], incs=[(vS, 1)])
            pl.op("vector", lambda e: e.tensor_scalar_mul(wun_sb, wn_bf,
                                                          smalls[:, SM_REC8:SM_REC8 + 1]),
                  waits=[(vS, vS.n)], incs=[(vS, 1)])
            v_w = vS.n
            if DEBUG and t == 1 and tt == 0:
                tap("t_wun", wun_sb, waits=[(vS, v_w)])
            for grp in range(2):
                b7 = banks[6 + grp]
                wsb = bank_w([b7], waits=[(vS, v_w)])
                for j in range(4):
                    nck = grp * 4 + j
                    pl.op("tensor", lambda e, nck=nck, j=j, b7=b7:
                          e.transpose(pbf[b7.idx][:, j * 128:(j + 1) * 128],
                                      wun_sb[:, nck * 128:(nck + 1) * 128], eye[:]),
                          waits=(wsb if j == 0 else ()),
                          incs=([(tS, 1)] if j == 3 else ()))
                t_tr = tS.n
                wun_guard[0] = (tS, t_tr)

                def ev(e, grp=grp, tt=tt, b7=b7):
                    dst = pt[:, MT:MT + 4096].rearrange("p (nck q) -> p nck q", q=512)
                    return e.copy(dst[:, grp * 4:(grp + 1) * 4, tt * 128:(tt + 1) * 128],
                                  pbf[b7.idx][:, 0:512])
                pl.op("scalar", ev, waits=[(tS, t_tr)], incs=[(aS, 1)])
                b7.guard = (aS, aS.n)
        mt_as = aS.n
        avr = dict(avr_pre)
        aux_as_last = None
        for ec in range(EC):
            grp, within = ec // 4, ec % 4
            bax = banks[ec % 4]
            ws = bank_w([bax], waits=[(dW, avr[grp % 2]), (aS, mt_as)])
            for nck in range(8):
                def mm(e, nck=nck, within=within, grp=grp, bax=bax):
                    off = (grp % 2) * 4096 + within * 1024 + nck * 128
                    return e.matmul(p[bax.idx][:], wbuf[:, off:off + 128],
                                    pt[:, MT + nck * 512: MT + nck * 512 + 512],
                                    start=(nck == 0), stop=(nck == 7))
                pl.op("tensor", mm, waits=(ws if nck == 0 else ()),
                      incs=([(tS, 1)] if nck == 7 else ()))
            if within == 3:
                wslot_guard[grp % 2] = (tS, tS.n)
                if grp + 2 < 8:
                    avr[grp % 2] = load_w(
                        grp % 2, av_h[(grp + 2) * 4:(grp + 3) * 4].transpose([1, 0, 2]))
            pl.op("scalar", lambda e, ec=ec, bax=bax:
                  e.copy(ar1[:, 16384 + ec * 512: 16384 + (ec + 1) * 512], p[bax.idx][:]),
                  waits=[(tS, tS.n)], incs=[(aS, 1)])
            bax.guard = (aS, aS.n)
            aux_as_last = aS.n
        mt_guard[0] = (tS, tS.n)
        wvd_dw = {0: load_w(0, wvd_h[0]), 1: load_w(1, wvd_h[1])}
        for m in range(2):
            bvd = banks[4 + m]
            ws = bank_w([bvd], waits=[(dW, wvd_dw[m]), (aS, aux_as_last)])
            for ec in range(EC):
                def mm(e, ec=ec, m=m, bvd=bvd):
                    return e.matmul(p[bvd.idx][:],
                                    wbuf[:, m * 4096 + ec * 128: m * 4096 + (ec + 1) * 128],
                                    ar1[:, 16384 + ec * 512: 16384 + (ec + 1) * 512],
                                    start=(ec == 0), stop=(ec == EC - 1))
                pl.op("tensor", mm, waits=(ws if ec == 0 else ()),
                      incs=([(tS, 1)] if ec == EC - 1 else ()))
            wslot_guard[m] = (tS, tS.n)
            pl.op("scalar", lambda e, m=m, t=t, bvd=bvd:
                  e.copy(botbf[:, t * 1024 + m * 512: t * 1024 + (m + 1) * 512], p[bvd.idx][:]),
                  waits=[(tS, tS.n)], incs=[(aS, 1)])
            bvd.guard = (aS, aS.n)
        if DEBUG:
            tap("t_bot", botbf[:, t * 1024:(t + 1) * 1024], idx=t, waits=[(aS, aS.n)])
        wu_dw = {0: load_w(0, wu1_h[0]), 1: load_w(1, wu1_h[1])}
        b6 = banks[6]
        ws_l = bank_w([b6])
        gbf_guard = [None, None]
        for uc in range(UC):
            bg = banks[uc % 2]
            ws = bank_w([bg], waits=[(dW, wu_dw[uc]), (vS, h1bf_written[t])])
            for fc in range(EC):
                def mm(e, fc=fc, uc=uc, bg=bg):
                    return e.matmul(p[bg.idx][:],
                                    wbuf[:, (uc % 2) * 4096 + fc * 128: (uc % 2) * 4096 + (fc + 1) * 128],
                                    ar1[:, fc * 512:(fc + 1) * 512],
                                    start=(fc == 0), stop=(fc == EC - 1))
                pl.op("tensor", mm, waits=(ws if fc == 0 else ()),
                      incs=([(tS, 1)] if fc == EC - 1 else ()))
            wslot_guard[uc % 2] = (tS, tS.n)
            if uc + 2 < UC:
                wu_dw[uc + 2] = load_w(uc % 2, wu1_h[uc + 2])
            t_u = tS.n
            ws = [gbf_guard[uc % 2]] if gbf_guard[uc % 2] else []
            pl.op("scalar", lambda e, uc=uc, bg=bg:
                  e.activation(pt[:, GBF + (uc % 2) * 512: GBF + (uc % 2 + 1) * 512],
                               p[bg.idx][:], ACT.Gelu_apprx_tanh, bias=bu1_sb[:, uc:uc + 1]),
                  waits=[(tS, t_u)] + ws, incs=[(aS, 1)])
            bg.guard = (aS, aS.n)
            a_g = aS.n
            pl.op("tensor", lambda e, uc=uc, b6=b6:
                  e.matmul(p[b6.idx][:], pt[:, W2R + uc * 128: W2R + (uc + 1) * 128],
                           pt[:, GBF + (uc % 2) * 512: GBF + (uc % 2 + 1) * 512],
                           start=(uc == 0), stop=(uc == UC - 1)),
                  waits=(ws_l if uc == 0 else []) + [(aS, a_g), (dW, wfq_dw)],
                  incs=([(tS, 1)] if uc == UC - 1 else ()))
            gbf_guard[uc % 2] = (tS, tS.n)
        pl.op("scalar", lambda e, t=t:
              e.activation(stats_sl("sig", t), p[b6.idx][:], ACT.Sigmoid,
                           bias=smalls[:, SM_BU2:SM_BU2 + 1], scale=1.0),
              waits=[(tS, tS.n), (dX, consts_dX)], incs=[(aS, 1)])
        b6.guard = (aS, aS.n)
        return aS.n

    gm_local = [None]

    def gmean_start(t0_v):
        pl.op("vector", lambda e:
              e.tensor_add(smalls[:, SM_GM:SM_GM + 1], smalls[:, SM_T0:SM_T0 + 1],
                           smalls[:, SM_T1:SM_T1 + 1]),
              waits=[(vS, t0_v)], incs=[(vS, 1)])
        gm_local[0] = [(vS, vS.n)]
        if not COLL:
            return
        dma("sync", cc_in_t[:, :], smalls[:, SM_GM:SM_GM + 1], dO, waits=[(vS, vS.n)])
        cc_in_d = dO.n
        pl.op("gpsimd", lambda e: e.collective_compute(
            "AllReduce", mybir.AluOpType.add, replica_groups=[list(range(8))],
            ins=[cc_in_t.ap().opt()], outs=[cc_out_t.ap().opt()]),
            waits=[(dO, cc_in_d)], incs=[(cS, 1)])

    def gmean_fetch():
        if not COLL:
            return gm_local[0]
        pl.op("gpsimd", lambda e: e.nop(cycle_cnt=2400), waits=[(cS, 1)], incs=[(cS, 1)])
        dma("gpsimd", smalls[:, SM_GM:SM_GM + 1], cc_out_t[:, :], dX, waits=[(cS, 2)])
        return [(dX, dX.n)]

    def gmean_combine(gm_ready):
        denom = 8192.0 if COLL else 1024.0
        pl.op("vector", lambda e, denom=denom:
              e.tensor_scalar(smalls[:, SM_GM:SM_GM + 1], smalls[:, SM_GM:SM_GM + 1],
                              1.0 / denom, 1e-6, ALU.mult, ALU.add),
              waits=gm_ready, incs=[(vS, 1)])
        pl.op("vector", lambda e: e.memset(smalls[:, SM_T0:SM_T0 + 1], 0.0),
              waits=[(vS, vS.n)], incs=[(vS, 1)])
        pl.op("vector", lambda e:
              e.reciprocal(smalls[:, SM_NVS:SM_NVS + 1], smalls[:, SM_GM:SM_GM + 1]),
              waits=[(vS, vS.n)], incs=[(vS, 1)])
        pl.op("vector", lambda e:
              e.tensor_scalar_mul(smalls[:, SM_NVS:SM_NVS + 1],
                                  smalls[:, SM_NVS:SM_NVS + 1], 0.5),
              waits=[(vS, vS.n)], incs=[(vS, 1)])
        return vS.n

    def phase_G(t, nv_v, h1_ready, bot_ready=(), g_first=()):
        pl.op("vector", lambda e, t=t:
              e.tensor_scalar_mul(SCR, stats_sl("logvar", t), smalls[:, SM_NVS:SM_NVS + 1]),
              waits=[(vS, nv_v)] + list(g_first), incs=[(vS, 1)])
        pl.op("vector", lambda e, t=t:
              e.scalar_tensor_tensor(GATE, stats_sl("sig", t), 2.5, SCR, ALU.mult, ALU.add),
              waits=[(vS, vS.n)], incs=[(vS, 1)])
        pl.op("vector", lambda e:
              e.tensor_scalar(GATE, GATE, 0.0, 5.0, ALU.max, ALU.min),
              waits=[(vS, vS.n)], incs=[(vS, 1)])
        pl.op("vector", lambda e:
              e.tensor_scalar(GATE, GATE, -0.5, 1.0 / 1.5, ALU.add, ALU.mult),
              waits=[(vS, vS.n)], incs=[(vS, 1)])
        pl.op("vector", lambda e:
              e.tensor_scalar(GATE, GATE, 0.0, 1.0, ALU.max, ALU.min),
              waits=[(vS, vS.n)], incs=[(vS, 1)])
        pl.op("scalar", lambda e:
              e.activation(GATE, GATE, ACT.Sigmoid, bias=smalls[:, SM_GB:SM_GB + 1],
                           scale=smalls[:, SM_GW1:SM_GW1 + 1]),
              waits=[(vS, vS.n)], incs=[(aS, 1)])
        pl.op("vector", lambda e:
              e.tensor_scalar(msk_sb[:, 0:512], GATE, 0.05, None, ALU.is_ge),
              waits=[(aS, aS.n)], incs=[(vS, 1)])
        pl.op("vector", lambda e:
              e.select(GATE, msk_sb[:, 0:512], GATE,
                       smalls[:, SM_ZERO:SM_ZERO + 1].to_broadcast([128, 512])),
              waits=[(vS, vS.n)], incs=[(vS, 1)])
        gate_v = vS.n
        if DEBUG:
            tap("t_gate", GATE, idx=t, waits=[(vS, gate_v)])
        for kc in range(2):
            pl.op("vector", lambda e, kc=kc, t=t:
                  e.tensor_mul(botbf[:, t * 1024 + kc * 512: t * 1024 + (kc + 1) * 512],
                               botbf[:, t * 1024 + kc * 512: t * 1024 + (kc + 1) * 512], GATE),
                  waits=([(vS, gate_v)] + list(bot_ready)) if kc == 0 else (),
                  incs=[(vS, 1)])
        botg_v = vS.n
        wvur = {0: load_w(0, wvu_h[0:8].transpose([1, 0, 2]), n_el=2048),
                1: load_w(1, wvu_h[8:16].transpose([1, 0, 2]), n_el=2048)}
        sb6 = bank_w([banks[6]])
        sb7 = bank_w([banks[7]])
        for ec in range(EC):
            grp, within = ec // 8, ec % 8
            bvu = banks[ec % 4]
            ws = bank_w([bvu], waits=[(dW, wvur[grp % 2])] +
                        ([(vS, h1_ready), (vS, botg_v)] if ec == 0 else []))
            for kc in range(2):
                def mm(e, ec=ec, kc=kc, t=t, bvu=bvu, grp=grp, within=within):
                    off = (grp % 2) * 4096 + within * 256 + kc * 128
                    return e.matmul(p[bvu.idx][:], wbuf[:, off:off + 128],
                                    botbf[:, t * 1024 + kc * 512: t * 1024 + (kc + 1) * 512],
                                    start=(kc == 0), stop=(kc == 1))
                pl.op("tensor", mm, waits=(ws if kc == 0 else ()),
                      incs=([(tS, 1)] if kc == 1 else ()))
            if within == 7:
                wslot_guard[grp % 2] = (tS, tS.n)
                if grp + 2 < 4:
                    wvur[grp % 2] = load_w(
                        grp % 2, wvu_h[(grp + 2) * 8:(grp + 3) * 8].transpose([1, 0, 2]), n_el=2048)
            t_vu = tS.n
            pl.op("vector", lambda e, ec=ec, bvu=bvu:
                  e.tensor_add(ar2f[:, ec * 512:(ec + 1) * 512],
                               ar2f[:, ec * 512:(ec + 1) * 512], p[bvu.idx][:]),
                  waits=[(tS, t_vu)], incs=[(vS, 1)])
            bvu.guard = (vS, vS.n)
            ws = [rbf_guard[ec % 2]] if rbf_guard[ec % 2] else []
            pl.op("vector", lambda e, ec=ec:
                  e.tensor_copy(pt[:, RBF + (ec % 2) * 512: RBF + (ec % 2 + 1) * 512],
                                ar2f[:, ec * 512:(ec + 1) * 512]),
                  waits=ws, incs=[(vS, 1)])
            pl.op("vector", lambda e, ec=ec:
                  e.tensor_mul(pt[:, SQ2 + (ec % 2) * 512: SQ2 + (ec % 2 + 1) * 512],
                               ar2f[:, ec * 512:(ec + 1) * 512],
                               ar2f[:, ec * 512:(ec + 1) * 512]),
                  incs=[(vS, 1)])
            v_sq = vS.n
            pl.op("tensor", lambda e, ec=ec:
                  e.matmul(p[6][:], ones[:], pt[:, RBF + (ec % 2) * 512: RBF + (ec % 2 + 1) * 512],
                           start=(ec == 0), stop=(ec == EC - 1)),
                  waits=(sb6 if ec == 0 else []) + [(vS, v_sq)])
            pl.op("tensor", lambda e, ec=ec:
                  e.matmul(p[7][:], ones[:], pt[:, SQ2 + (ec % 2) * 512: SQ2 + (ec % 2 + 1) * 512],
                           start=(ec == 0), stop=(ec == EC - 1)),
                  waits=(sb7 if ec == 0 else []),
                  incs=[(tS, 1)])
            rbf_guard[ec % 2] = (tS, tS.n)
        t_stats = tS.n
        if DEBUG:
            tap("t_r2", ar2f[:], idx=t, waits=[(vS, vS.n)])
        pl.op("vector", lambda e: e.tensor_scalar_mul(M2, p[6][:], 1.0 / E),
              waits=[(tS, t_stats)], incs=[(vS, 1)])
        pl.op("vector", lambda e: e.tensor_scalar_mul(SCR, p[7][:], 1.0 / E), incs=[(vS, 1)])
        banks[6].guard = (vS, vS.n)
        banks[7].guard = (vS, vS.n)
        pl.op("vector", lambda e: e.tensor_mul(RSTD2, M2, M2),
              waits=[(vS, vS.n)], incs=[(vS, 1)])
        pl.op("vector", lambda e: e.tensor_sub(SCR, SCR, RSTD2),
              waits=[(vS, vS.n)], incs=[(vS, 1)])
        pl.op("scalar", lambda e: e.activation(SCR, SCR, ACT.Sqrt,
                                               bias=smalls[:, SM_EPS:SM_EPS + 1]),
              waits=[(vS, vS.n)], incs=[(aS, 1)])
        pl.op("vector", lambda e: e.reciprocal(RSTD2, SCR), waits=[(aS, aS.n)], incs=[(vS, 1)])
        ln2_w = (vS, vS.n)
        stg_guard = [None, None]
        sub_marks = []
        for ec in range(EC):
            ws = [ln2_w] if ec == 0 else []
            pl.op("vector", lambda e, ec=ec:
                  e.tensor_sub(SCR, ar2f[:, ec * 512:(ec + 1) * 512], M2),
                  waits=ws, incs=[(vS, 1)])
            sub_marks.append(vS.n)
            ws = [stg_guard[ec % 2]] if stg_guard[ec % 2] else []
            pl.op("vector", lambda e, ec=ec:
                  e.tensor_mul(ptf[:, GSTGf + (ec % 2) * 512: GSTGf + (ec % 2 + 1) * 512],
                               SCR, RSTD2),
                  waits=ws, incs=[(vS, 1)])
            v_o = vS.n
            dma("sync", y_h[ec, :, t * 512:(t + 1) * 512],
                ptf[:, GSTGf + (ec % 2) * 512: GSTGf + (ec % 2 + 1) * 512], dO,
                waits=[(vS, v_o)])
            stg_guard[ec % 2] = (dO, dO.n)
        return sub_marks

    r1v_0 = phase_D(0, spill=True, first_waits=[(vS, attn_done_v)] + pop_tap())
    spill_r1_dO = dO.n
    r1v_1 = phase_D(1, spill=False, first_waits=[(dO, spill_r1_dO)])
    gmean_start(r1v_1)
    if DEBUG:
        tap("t_r1", ar2f[:], idx=1, waits=[(vS, r1v_1)])
    h1v_1, e1_marks = phase_E(1, [(vS, r1v_1)] + pop_tap())
    if DEBUG:
        tap("t_h1", ar2f[:], idx=1, waits=[(vS, h1v_1)])
    _tg = pop_tap()
    sp_d = []
    for i in range(8):
        dma("sync", spill_h1[i * 4:(i + 1) * 4].transpose([1, 0, 2]),
            ar2f[:, i * 2048:(i + 1) * 2048], dO,
            waits=[(vS, e1_marks[i * 4 + 3])] + (_tg if i == 0 else []))
        sp_d.append(dO.n)
    pre1 = preload_F()
    for i in range(8):
        dma("gpsimd", ar2f[:, i * 2048:(i + 1) * 2048],
            spill_r1[i * 4:(i + 1) * 4].transpose([1, 0, 2]), dX,
            waits=[(dO, sp_d[i])])
    r1_reload = dX.n
    f1_done = phase_F(1, pre=pre1)
    if DEBUG:
        tap("t_r1", ar2f[:], idx=0, waits=[(dX, r1_reload)])
    h1v_0, _ = phase_E(0, [(dX, r1_reload), (vS, r1v_0)] + pop_tap())
    if DEBUG:
        tap("t_h1", ar2f[:], idx=0, waits=[(vS, h1v_0)])
    _tg0 = pop_tap()
    f0_done = phase_F(0)
    _ = _tg0
    _tgG = pop_tap()
    gm_ready = gmean_fetch()
    nv_v = gmean_combine(gm_ready)
    g0_subs = phase_G(0, nv_v, h1v_0, bot_ready=[(aS, f0_done)], g_first=_tgG)
    for i in range(8):
        dma("gpsimd", ar2f[:, i * 2048:(i + 1) * 2048],
            spill_h1[i * 4:(i + 1) * 4].transpose([1, 0, 2]), dX,
            waits=[(vS, g0_subs[i * 4 + 3])])
    h1_reload = dX.n
    pl.op("vector", lambda e: e.memset(smalls[:, SM_NM:SM_NM + 1], 0.0),
          waits=[(dX, h1_reload)], incs=[(vS, 1)])
    phase_G(1, nv_v, vS.n, bot_ready=[(aS, f1_done)], g_first=pop_tap())

    dma("sync", dsm_h, smalls[:], dO, waits=[(vS, vS.n), (aS, aS.n)])
    pl.op("sync", None,
          waits=[(dO, dO.n), (dW, dW.n), (dX, dX.n), (tS, tS.n), (vS, vS.n), (aS, aS.n)])

    with (nc.semaphore("sdW") as h_dW, nc.semaphore("sdX") as h_dX,
          nc.semaphore("sdO") as h_dO, nc.semaphore("stS") as h_tS,
          nc.semaphore("svS") as h_vS, nc.semaphore("saS") as h_aS,
          nc.semaphore("shA") as h_hA, nc.semaphore("shB") as h_hB,
          nc.semaphore("scS") as h_cS):
        handles = {"dW": h_dW, "dX": h_dX, "dO": h_dO, "tS": h_tS, "vS": h_vS, "aS": h_aS,
                   "hA": h_hA, "hB": h_hB, "cS": h_cS}
        for s in pl.sems:
            s.h = handles[s.name]
        with nc.Block() as block:
            def make(eng_name):
                def body(eng):
                    for e_name, fn, waits, incs in pl.ops:
                        if e_name != eng_name:
                            continue
                        for s, v in waits:
                            eng.wait_ge(s.h, v)
                        if fn is None:
                            continue
                        inst = fn(eng)
                        for s, a in incs:
                            inst = inst.then_inc(s.h, a)
                return body
            block.sync(make("sync"))
            block.gpsimd(make("gpsimd"))
            block.tensor(make("tensor"))
            block.vector(make("vector"))
            block.scalar(make("scalar"))
    es.close()
    return nc


_NC_CACHE = [None]


def _prep_shared(wd):
    out = {}
    out["wkv"] = np.concatenate([tile_w(wd["Wk"].T), tile_w(wd["Wv"].T)], axis=0)
    out["wq"] = tile_w(wd["Wq"].T)
    out["wo"] = tile_w(wd["Wo"].T)
    out["wu1"] = tile_w(wd["Wu1"].T)
    out["wfq"] = tile_w64(wd["Wfq"].T)
    out["wvd"] = tile_w(wd["Wvd"].T)
    out["wvu"] = tile_w(wd["Wvu"].T)
    out["wr"] = np.ascontiguousarray(wd["Wr"].T)
    out["akt"] = np.ascontiguousarray(wd["ak"].T)
    out["av"] = tile_w(wd["avv"])
    out["w2r"] = np.ascontiguousarray(np.tile(wd["Wu2"].reshape(8, 128, 1), (1, 1, 128)))
    out["bu1c"] = np.ascontiguousarray(wd["bu1"].reshape(8, 128).T)
    lr = wd["lrel"].reshape(1, 1024)
    lr_hi = lr.astype(BF).astype(np.float32)
    out["lrel"] = np.ascontiguousarray(np.concatenate([lr_hi, lr - lr_hi], axis=0))
    out["bu2r"] = np.tile(wd["bu2"].reshape(1, 1), (128, 1)).astype(np.float32)
    out["gw1r"] = np.full((128, 1), np.float32(wd["gw1"]), dtype=np.float32)
    out["gbr"] = np.full((128, 1), np.float32(wd["gb"]), dtype=np.float32)
    out["eyec"] = np.eye(128, dtype=np.float32).astype(BF)
    out["onesc"] = np.ones((128, 128), dtype=BF)
    p64_np = np.zeros((128, 128), dtype=np.float32)
    p64_np[(np.arange(128) + 64) % 128, np.arange(128)] = 1.0
    out["p64c"] = p64_np.astype(BF)
    out["ones1c"] = np.ones((1, 512), dtype=BF)
    return out


def _prep_core(x, b, h):
    posq, poskv = perm_positions(h)
    d = {}
    xb = x[b]
    d["xq"] = np.ascontiguousarray(xb[posq].T.reshape(EC, 128, 1024))
    d["xkv"] = np.ascontiguousarray(xb[poskv].T.reshape(EC, 128, 1024))
    d["cosq"], d["sinq"] = rope_tables(posq, 1.0 / math.sqrt(D))
    d["cosk"], d["sink"] = rope_tables(poskv, 1.0)
    k_i = np.arange(128)[:, None]
    q_i = np.arange(128)[None, :]
    if h == 0:
        m = np.where(k_i >= q_i, 0.0, NEG).astype(np.float32)
    else:
        m = np.where(q_i >= k_i, 0.0, NEG).astype(np.float32)
    d["maskc"] = m.astype(BF)
    return d


def kernel(L0_attn_hidden_states, L0_attn_Wq, L0_attn_Wk, L0_attn_Wv, L0_attn_Wo,
           L0_ffn_Wq, L0_ffn_Wr, L0_ffn_aux_keys, L0_ffn_aux_values, L0_ffn_Wvd,
           L0_ffn_Wvu, L0_ffn_Wu1, L0_ffn_bu1, L0_ffn_Wu2, L0_ffn_bu2,
           L0_ffn_gate_w1, L0_ffn_gate_bias, L0_ffn_log_rel):
    f = lambda a: np.asarray(a, dtype=np.float32)
    x = f(L0_attn_hidden_states)
    wd = dict(Wq=f(L0_attn_Wq), Wk=f(L0_attn_Wk), Wv=f(L0_attn_Wv), Wo=f(L0_attn_Wo),
              Wfq=f(L0_ffn_Wq), Wr=f(L0_ffn_Wr), ak=f(L0_ffn_aux_keys),
              avv=f(L0_ffn_aux_values), Wvd=f(L0_ffn_Wvd), Wvu=f(L0_ffn_Wvu),
              Wu1=f(L0_ffn_Wu1), bu1=f(L0_ffn_bu1), Wu2=f(L0_ffn_Wu2),
              bu2=f(L0_ffn_bu2), gw1=f(L0_ffn_gate_w1), gb=f(L0_ffn_gate_bias),
              lrel=f(L0_ffn_log_rel))
    shared = _prep_shared(wd)
    in_maps = []
    for c in range(8):
        b, h = c // 2, c % 2
        m = dict(shared)
        m.update(_prep_core(x, b, h))
        in_maps.append(m)
    if _NC_CACHE[0] is None:
        _NC_CACHE[0] = build_nc()
    nc = _NC_CACHE[0]
    res = run_bass_kernel_spmd(nc, in_maps, core_ids=list(range(8)))
    out = np.empty((B, S, E), dtype=np.float32)
    for c in range(8):
        b, h = c // 2, c % 2
        posq, _ = perm_positions(h)
        y = np.asarray(res.results[c]["y"]).reshape(E, 1024)
        out[b, posq, :] = y.T
    return out
